# version 15
# baseline (speedup 1.0000x reference)
import numpy as np
import ml_dtypes
from contextlib import ExitStack

import concourse.bass as bass
import concourse.mybir as mybir
from concourse.bass_utils import run_bass_kernel_spmd

F32 = mybir.dt.float32
BF16 = mybir.dt.bfloat16
FP8 = mybir.dt.float8e4
AF = mybir.ActivationFunctionType
ALU = mybir.AluOpType
AX = mybir.AxisListType

D = 1024


class Sched:

    ENGS = ("sync", "gpsimd", "vector", "scalar", "tensor")

    def __init__(self):
        self.ops = {e: [] for e in self.ENGS}
        self.count = {}

    def op(self, eng, fn, inc=None, amt=1):
        self.ops[eng].append(("op", fn, inc, amt))
        if inc is not None:
            self.count[inc] = self.count.get(inc, 0) + amt
            return self.count[inc]
        return None

    def drain(self, eng):
        self.ops[eng].append(("op", lambda e: e.drain(), None, 1))

    def wait(self, eng, sem_name, val):
        if val > 0:
            self.ops[eng].append(("wait", sem_name, val))

    def cnt(self, s):
        return self.count.get(s, 0)

    def emit(self, eng_obj, eng_name, sems):
        for item in self.ops[eng_name]:
            if item[0] == "wait":
                eng_obj.wait_ge(sems[item[1]], item[2])
            else:
                _, fn, inc, amt = item
                inst = fn(eng_obj)
                if inc is not None:
                    inst.then_inc(sems[inc], amt)


def build_core(T=4096, NB_REST=7, debug_stop=None):
    NT = T // 128
    NMAC = T // 256
    TR = NB_REST * T
    NTR = TR // 128
    NTOT = NT + NTR

    nc = bass.Bass("TRN2", target_bir_lowering=False)
    dram = lambda n, s, dt, k: nc.dram_tensor(n, s, dt, kind=k).ap()
    x_own = dram("x_own", [T, D], F32, "ExternalInput")
    x_rest = dram("x_rest", [TR, D], F32, "ExternalInput")
    wt = {n: dram("wt_" + n, [D, D], F32, "ExternalInput")
          for n in ("q", "k", "v", "g", "o", "f")}
    b_c = {n: dram("b_" + n + "_c", [128, 8], F32, "ExternalInput")
           for n in ("q", "k", "g")}
    b_r = {n: dram("b_" + n + "_r", [1, D], F32, "ExternalInput")
           for n in ("v", "o", "f")}
    gamma_d = dram("gamma_c", [128, 8], F32, "ExternalInput")
    beta_d = dram("beta_c", [128, 8], F32, "ExternalInput")
    alpha_d = dram("alpha2h", [128, 8], F32, "ExternalInput")
    ident32_d = dram("ident32", [128, 128], F32, "ExternalInput")
    identbf_d = dram("identbf", [128, 128], BF16, "ExternalInput")
    e_q_d = dram("e_q", [128, 128], F32, "ExternalInput")
    e_k_d = dram("e_k", [128, 128], F32, "ExternalInput")
    e_k3_d = dram("e_k3", [128, 128], F32, "ExternalInput")
    maskut_d = dram("maskut", [128, 128], F32, "ExternalInput")
    ones_bf_d = dram("ones_bf", [1, 256], BF16, "ExternalInput")
    ones_f32_d = dram("ones_f32", [1, 128], F32, "ExternalInput")
    y_out = dram("y_out", [T, D], F32, "ExternalOutput")

    S = Sched()
    sem_names = ["lc", "lb", "lda0", "lda1", "lda2", "lda3",
                 "lxa", "lxb", "gl", "pe", "dv", "sc", "gp", "sta", "stb",
                 "wfa", "wfb"]

    with ExitStack() as ctx:
        sb = lambda n, s, dt: ctx.enter_context(nc.sbuf_tensor(n, s, dt))
        pst = lambda n, s, dt: ctx.enter_context(nc.psum_tensor(n, s, dt))

        Wb = {n: sb("W" + n, [128, 8192], BF16) for n in ("q", "k", "v", "g", "o")}
        WFc = sb("WFc", [128, 8192], FP8)
        XIN = [sb("XIN0", [128, 2048], F32), sb("XIN1", [128, 2048], F32)]
        XN = sb("XN", [128, 2048], F32)
        D2T = sb("D2T", [128, 256], F32)
        D1 = sb("D1", [128, 128], F32)
        ZT = sb("ZT", [128, 2048], BF16)
        XQ = sb("XQ", [128, 2048], FP8)
        Q2 = sb("Q2", [128, 2048], BF16)
        K2BD = sb("K2BD", [128, 4096], BF16)
        K3F = sb("K3F", [128, 2048], BF16)
        K3T = sb("K3T", [128, 2048], BF16)
        VT = sb("VT", [128, 2048], BF16)
        GT = sb("GT", [128, 2048], BF16)
        AM = sb("AM", [128, 512], BF16)
        OG = sb("OG", [128, 2048], BF16)
        HQB = sb("HQB", [128, 1024], BF16)
        TR1 = [sb("TR1a", [128, 512], F32), sb("TR1b", [128, 512], F32)]
        FB = sb("FB", [128, 2048], F32)
        OUT = sb("OUT", [128, 2048], F32)
        PA = [OUT[:, 0:1024], OUT[:, 1024:2048], FB[:, 0:1024], FB[:, 1024:2048]]
        st_sq1 = sb("st_sq1", [128, 256], F32)
        st_sq2 = sb("st_sq2", [128, 256], F32)
        st_mx = sb("st_mx", [128, 256], F32)
        st_mn = sb("st_mn", [128, 256], F32)
        mu1c = sb("mu1c", [128, 32], F32)
        r1c = sb("r1c", [128, 32], F32)
        r2c = sb("r2c", [128, 32], F32)
        r12c = sb("r12c", [128, 32], F32)
        id32 = sb("id32", [128, 128], F32)
        idbf = sb("idbf", [128, 128], BF16)
        EtQ = sb("EtQ", [128, 128], F32)
        EtK = sb("EtK", [128, 128], F32)
        EtK3 = sb("EtK3", [128, 128], F32)
        mUT = sb("mUT", [128, 128], F32)
        onebf = sb("onebf", [1, 256], BF16)
        onef = sb("onef", [1, 128], F32)
        onecol = sb("onecol", [128, 1], F32)
        gam = sb("gam", [128, 8], F32)
        bet = sb("bet", [128, 8], F32)
        betbf = sb("betbf", [128, 8], BF16)
        alog = sb("alog", [128, 8], F32)
        Mq2 = sb("Mq2", [128, 1024], BF16)
        Mk2 = sb("Mk2", [128, 1024], BF16)
        Mk3 = sb("Mk3", [128, 1024], BF16)
        bqr = sb("bqr", [1, 1024], BF16)
        bkr = sb("bkr", [1, 1024], BF16)
        bgr = sb("bgr", [1, 1024], BF16)
        a64c = sb("a64c", [128, 8], F32)
        a128c = sb("a128c", [128, 8], F32)
        lnac = sb("lnac", [128, 8], F32)
        bqp = sb("bqp", [128, 8], F32)
        bkp = sb("bkp", [128, 8], F32)
        bgp = sb("bgp", [128, 8], F32)
        bvb = sb("bvb", [1, 1024], BF16)
        bob = sb("bob", [1, 1024], BF16)
        bfrb = sb("bfrb", [1, 1024], BF16)
        tcol = sb("tcol", [128, 8], F32)
        trow = sb("trow", [1, 128], F32)
        sca = sb("sca", [1, 8], F32)
        sxcol = sb("sxcol", [128, 1], F32)
        swcol = sb("swcol", [128, 1], F32)
        ccol = sb("ccol", [128, 1], F32)
        ZSL = [XN[:, 0:1024], XN[:, 1024:2048],
               XIN[0][:, 0:1024], XIN[0][:, 1024:2048]]
        brow = FB[0:1, 0:1024]
        T256a = FB[:, 0:256]
        T256b = FB[:, 256:512]

        p0 = pst("p0", [128, 512], F32)
        p1 = pst("p1", [128, 512], F32)
        pta = pst("pta", [128, 1024], BF16)
        ptb = pst("ptb", [128, 1024], BF16)
        p4 = pst("p4", [128, 512], F32)
        p5 = pst("p5", [128, 512], F32)
        p6 = pst("p6", [128, 512], F32)
        p7 = pst("p7", [128, 512], F32)

        sems = {n: ctx.enter_context(nc.semaphore(n)) for n in sem_names}
        block = ctx.enter_context(nc.Block())

        def _emit_all():
            @block.sync
            def _(eng):
                S.emit(eng, "sync", sems)

            @block.gpsimd
            def _(eng):
                S.emit(eng, "gpsimd", sems)

            @block.vector
            def _(eng):
                S.emit(eng, "vector", sems)

            @block.scalar
            def _(eng):
                S.emit(eng, "scalar", sems)

            @block.tensor
            def _(eng):
                S.emit(eng, "tensor", sems)

        LD = lambda dst, src: S.op(
            "sync", lambda e, d=dst, s=src: e.dma_start(out=d, in_=s), "lc", 16)
        LDA = ["lda0", "lda1", "lda2", "lda3"]

        def warm(sem_name=None, val=0):
            if sem_name is not None and val > 0:
                S.wait("tensor", sem_name, val)
            S.op("tensor", lambda e: e.matmul(
                p7[0:64, 448:512], onebf[0:1, 0:64], onebf[0:1, 64:128],
                start=True, stop=True))

        for dst, src in ((id32[:], ident32_d), (idbf[:], identbf_d),
                         (EtQ[:], e_q_d), (EtK[:], e_k_d), (EtK3[:], e_k3_d),
                         (mUT[:], maskut_d), (onebf[:], ones_bf_d),
                         (onef[:], ones_f32_d), (gam[:], gamma_d),
                         (bet[:], beta_d), (alog[:], alpha_d),
                         (bqp[:], b_c["q"]), (bkp[:], b_c["k"]),
                         (bgp[:], b_c["g"])):
            LD(dst, src)
        ld_consts = S.cnt("lc")
        for n in ("q", "k", "v", "g", "o"):
            for kk in range(8):
                S.op("gpsimd", lambda e, n=n, kk=kk: e.dma_start(
                    out=Wb[n][:, kk * 1024:(kk + 1) * 1024],
                    in_=wt[n][kk * 128:(kk + 1) * 128, :]), "gl", 16)
        gl_w = S.cnt("gl")
        S.op("gpsimd", lambda e: e.memset(HQB[:], 0.0))
        S.op("gpsimd", lambda e: e.memset(AM[:], 0.0))
        S.op("gpsimd", lambda e: e.memset(K2BD[:], 0.0))
        S.op("gpsimd", lambda e: e.memset(onecol[:], 1.0), "gp")
        gp_init = S.cnt("gp")

        S.wait("scalar", "lc", ld_consts)
        S.op("scalar", lambda e: e.activation(out=tcol[:], in_=alog[:], func=AF.Sigmoid))
        S.drain("scalar")
        S.op("scalar", lambda e: e.activation(out=lnac[:], in_=tcol[:], func=AF.Ln),
             "sc")
        sc_lna = S.cnt("sc")
        S.wait("vector", "lc", ld_consts)
        S.wait("vector", "sc", sc_lna)
        for g in range(8):
            for j, (M, E) in enumerate(((Mq2, EtQ), (Mk2, EtK), (Mk3, EtK3))):
                half = (g * 3 + j) % 2
                if g * 3 + j >= 2:
                    S.wait("vector", "sc", S.cnt("sc"))
                dvx = S.op("vector", lambda e, g=g, E=E, half=half: e.tensor_scalar(
                    out=TR1[0][:, half * 128:(half + 1) * 128], in0=E[:],
                    scalar1=lnac[:, g:g + 1], scalar2=None, op0=ALU.mult), "dv")
                S.wait("scalar", "dv", dvx)
                S.op("scalar", lambda e, g=g, M=M, half=half: e.activation(
                    out=M[:, g * 128:(g + 1) * 128],
                    in_=TR1[0][:, half * 128:(half + 1) * 128], func=AF.Exp), "sc")
        S.op("vector", lambda e: e.tensor_scalar(
            out=a64c[:], in0=lnac[:], scalar1=64.0, scalar2=None, op0=ALU.mult))
        S.op("vector", lambda e: e.tensor_scalar(
            out=a128c[:], in0=lnac[:], scalar1=128.0, scalar2=None, op0=ALU.mult),
            "dv")
        dv_m = S.cnt("dv")
        S.wait("scalar", "dv", dv_m)
        S.op("scalar", lambda e: e.activation(out=a64c[:], in_=a64c[:], func=AF.Exp))
        S.op("scalar", lambda e: e.activation(out=a128c[:], in_=a128c[:], func=AF.Exp),
             "sc")
        sc_alpha = S.cnt("sc")

        S.op("vector", lambda e: e.tensor_copy(out=betbf[:], in_=bet[:]), "dv")
        dv_bet = S.cnt("dv")

        S.wait("tensor", "gl", gl_w)
        S.wait("tensor", "dv", dv_bet)
        for n, dstc in (("q", bqp), ("k", bkp), ("g", bgp)):
            pe_n = None
            for g in range(8):
                for kk in range(8):
                    last = (g == 7 and kk == 7)
                    pe_n = S.op("tensor", lambda e, n=n, g=g, kk=kk: e.matmul(
                        p0[:, g:g + 1],
                        Wb[n][:, kk * 1024 + g * 128: kk * 1024 + (g + 1) * 128],
                        betbf[:, kk:kk + 1],
                        start=(kk == 0), stop=(kk == 7)),
                        "pe" if last else None)
            S.wait("vector", "pe", pe_n)
            S.op("vector", lambda e, dstc=dstc: e.tensor_tensor(
                out=dstc[:], in0=p0[:, 0:8], in1=dstc[:], op=ALU.add), "dv")
            S.wait("tensor", "dv", S.cnt("dv"))
        for colt, rowt in ((bqp, bqr), (bkp, bkr), (bgp, bgr)):
            for g in range(8):
                S.wait("tensor", "dv", S.cnt("dv"))
                pt = S.op("tensor", lambda e, colt=colt, g=g: e.transpose(
                    p6[0:1, 0:128], colt[:, g:g + 1], id32[:]), "pe")
                S.wait("vector", "pe", pt)
                S.op("vector", lambda e, rowt=rowt, g=g: e.tensor_copy(
                    out=rowt[0:1, g * 128:(g + 1) * 128], in_=p6[0:1, 0:128]), "dv")
        S.op("sync", lambda e: e.dma_start(out=brow, in_=b_r["v"]), "lb", 16)
        ld_bv = S.cnt("lb")
        for half in range(2):
            pe_v = None
            for kk in range(8):
                pe_v = S.op("tensor", lambda e, kk=kk, half=half: e.matmul(
                    p1[0:1, 0:512],
                    betbf[:, kk:kk + 1],
                    Wb["v"][:, kk * 1024 + half * 512: kk * 1024 + (half + 1) * 512],
                    start=(kk == 0), stop=(kk == 7)),
                    "pe" if kk == 7 else None)
            S.wait("vector", "pe", pe_v)
            S.wait("vector", "lb", ld_bv)
            S.op("vector", lambda e, half=half: e.scalar_tensor_tensor(
                out=bvb[0:1, half * 512:(half + 1) * 512], in0=p1[0:1, 0:512],
                scalar=1.0, in1=brow[0:1, half * 512:(half + 1) * 512],
                op0=ALU.mult, op1=ALU.add), "dv")
            S.wait("tensor", "dv", S.cnt("dv"))
        S.wait("sync", "dv", S.cnt("dv"))
        S.wait("sync", "lb", S.cnt("lb"))
        S.op("sync", lambda e: e.dma_start(out=brow, in_=b_r["o"]), "lb", 16)
        ld_bo = S.cnt("lb")
        S.wait("vector", "lb", ld_bo)
        S.op("vector", lambda e: e.tensor_copy(out=bob[:], in_=brow), "dv")
        dv_bob = S.cnt("dv")
        S.wait("gpsimd", "gl", S.cnt("gl"))
        S.op("gpsimd", lambda e: e.dma_start(out=bfrb[:], in_=b_r["f"]), "gl", 16)
        gl_bf = S.cnt("gl")

        for n in ("q", "k", "v", "g"):
            for kk in range(8):
                S.op("vector", lambda e, n=n, kk=kk: e.tensor_scalar(
                    out=Wb[n][:, kk * 1024:(kk + 1) * 1024],
                    in0=Wb[n][:, kk * 1024:(kk + 1) * 1024],
                    scalar1=gam[:, kk:kk + 1], scalar2=None, op0=ALU.mult))
        S.op("vector", lambda e: e.engine_nop(), "dv")
        dv_fold = S.cnt("dv")

        WFBUF = [XIN[1][:, 0:1024], XIN[1][:, 1024:2048]]
        WFS = ["wfa", "wfb"]
        for kk in range(8):
            buf = WFBUF[kk % 2]
            sld = WFS[kk % 2]
            if kk >= 2:
                S.wait("sync", "dv", S.cnt("dv"))
            S.op("sync", lambda e, kk=kk, buf=buf: e.dma_start(
                out=buf, in_=wt["f"][kk * 128:(kk + 1) * 128, :]), sld, 16)
            warm(sld, S.cnt(sld))
            S.wait("vector", sld, S.cnt(sld))
            if kk == 0:
                S.op("vector", lambda e, buf=buf: e.reduce_max(
                    out=tcol[:, 0:1], in_=buf, axis=AX.X,
                    apply_absolute_value=True), "dv")
            else:
                S.op("vector", lambda e, buf=buf: e.reduce_max(
                    out=tcol[:, 1:2], in_=buf, axis=AX.X,
                    apply_absolute_value=True))
                S.drain("vector")
                S.op("vector", lambda e: e.tensor_tensor(
                    out=tcol[:, 0:1], in0=tcol[:, 0:1], in1=tcol[:, 1:2],
                    op=ALU.max), "dv")
        S.wait("tensor", "dv", S.cnt("dv"))
        S.op("tensor", lambda e: e.transpose(p6[0:1, 0:128], tcol[:, 0:1], id32[:]),
             "pe")
        S.wait("vector", "pe", S.cnt("pe"))
        S.op("vector", lambda e: e.tensor_copy(out=trow[:], in_=p6[0:1, 0:128]))
        S.drain("vector")
        S.op("vector", lambda e: e.reduce_max(out=sca[0:1, 1:2], in_=trow[:], axis=AX.X))
        S.drain("vector")
        S.op("vector", lambda e: e.tensor_scalar(
            out=sca[0:1, 2:3], in0=sca[0:1, 1:2], scalar1=1e-12, scalar2=None,
            op0=ALU.max))
        S.drain("vector")
        S.op("vector", lambda e: e.reciprocal(out=sca[0:1, 3:4], in_=sca[0:1, 2:3]))
        S.drain("vector")
        S.op("vector", lambda e: e.tensor_scalar(
            out=sca[0:1, 3:4], in0=sca[0:1, 3:4], scalar1=224.0, scalar2=None,
            op0=ALU.mult), "dv")
        S.wait("tensor", "dv", S.cnt("dv"))
        S.op("tensor", lambda e: e.matmul(p6[:, 0:1], onef[:], sca[0:1, 3:4],
                                          start=True, stop=True), "pe")
        S.wait("vector", "pe", S.cnt("pe"))
        S.op("vector", lambda e: e.tensor_copy(out=swcol[:], in_=p6[:, 0:1]), "dv")
        S.drain("vector")
        dv_swc = S.cnt("dv")
        for kk in range(8):
            buf = WFBUF[kk % 2]
            sld = WFS[kk % 2]
            S.wait("sync", "dv", S.cnt("dv") if kk >= 2 else dv_swc)
            S.op("sync", lambda e, kk=kk, buf=buf: e.dma_start(
                out=buf, in_=wt["f"][kk * 128:(kk + 1) * 128, :]), sld, 16)
            warm(sld, S.cnt(sld))
            S.wait("vector", sld, S.cnt(sld))
            S.op("vector", lambda e, kk=kk, buf=buf: e.tensor_scalar(
                out=WFc[:, kk * 1024:(kk + 1) * 1024], in0=buf,
                scalar1=swcol[:], scalar2=None, op0=ALU.mult), "dv")
        dv_wfq = S.cnt("dv")

        if debug_stop == "setup":
            S.wait("sync", "dv", S.cnt("dv"))
            S.wait("sync", "sc", S.cnt("sc"))
            S.wait("sync", "pe", S.cnt("pe"))
            S.wait("sync", "gp", S.cnt("gp"))
            for i in range(T // 128):
                S.wait("sync", "sta", 16 * i)
                S.op("sync", lambda e, i=i: e.dma_start(
                    out=y_out[i * 128:(i + 1) * 128, :], in_=XN[:, 0:1024]),
                    "sta", 16)
            S.wait("sync", "sta", S.cnt("sta"))
            _emit_all()
            return nc

        S.wait("scalar", "sc", sc_alpha)
        S.wait("scalar", "gp", gp_init)
        S.wait("gpsimd", "gp", gp_init)
        gp_tree = [0] * NTOT
        dv_red = [0] * NTOT
        sc_z = [0] * NTOT
        dv_max = [0] * NTOT
        for i in range(NTOT):
            buf = PA[i % 4]
            sld = LDA[i % 4]
            if i == 0:
                pass
            if i >= 4:
                S.wait("sync", "gp", gp_tree[i - 4])
                S.wait("sync", "sc", sc_z[i - 4])
            src = (x_own[(i) * 128:(i + 1) * 128, :] if i < NT
                   else x_rest[(i - NT) * 128:(i - NT + 1) * 128, :])
            S.op("sync", lambda e, buf=buf, src=src: e.dma_start(out=buf, in_=src),
                 sld, 16)
            ldv = S.cnt(sld)
            S.wait("gpsimd", sld, ldv)
            if i < 2:
                S.wait("gpsimd", "sc", sc_alpha)
            if i >= 2:
                S.wait("gpsimd", "dv", dv_red[i - 2])
            gp_tree[i] = S.op("gpsimd", lambda e, buf=buf, i=i: e.tensor_tensor(
                out=TR1[i % 2][:], in0=buf[:, 0:512], in1=buf[:, 512:1024],
                op=ALU.add), "gp")
            S.wait("vector", "gp", gp_tree[i])
            dv_red[i] = S.op("vector", lambda e, i=i: e.reduce_sum(
                out=st_sq1[:, i:i + 1], in_=TR1[i % 2][:], axis=AX.X), "dv")
            if i >= 2:
                j = i - 2
                S.wait("vector", "sc", sc_z[j])
                dv_max[j] = S.op("vector", lambda e, j=j: e.reduce_max(
                    out=st_mx[:, j:j + 1], in_=ZSL[j % 4], axis=AX.X), "dv")
            S.wait("scalar", "dv", dv_red[i])
            S.wait("scalar", sld, ldv)
            if i >= 4:
                S.wait("scalar", "dv", dv_max[i - 4])
            sc_z[i] = S.op("scalar", lambda e, buf=buf, i=i: e.activation(
                out=ZSL[i % 4], in_=buf, func=AF.Square,
                bias=st_sq1[:, i:i + 1], scale=-1024.0,
                accum_out=st_sq2[:, i:i + 1]), "sc")
            warm(sld, ldv)
        for j in range(NTOT - 2, NTOT):
            S.wait("vector", "sc", sc_z[j])
            dv_max[j] = S.op("vector", lambda e, j=j: e.reduce_max(
                out=st_mx[:, j:j + 1], in_=ZSL[j % 4], axis=AX.X), "dv")
        sc_passA = S.cnt("sc")
        dv_passA = S.cnt("dv")

        NW = NTOT
        S.wait("vector", "sc", sc_passA)
        S.drain("vector")
        S.op("vector", lambda e: e.tensor_scalar(
            out=T256a[:, 0:NW], in0=st_sq1[:, 0:NW], scalar1=1.0 / 1024.0,
            scalar2=None, op0=ALU.mult))
        S.drain("vector")
        S.op("vector", lambda e: e.tensor_scalar(
            out=T256b[:, 0:NW], in0=st_sq2[:, 0:NW],
            scalar1=1.0 / (1024.0 * 1024.0 * 1024.0),
            scalar2=None, op0=ALU.mult))
        S.drain("vector")
        S.op("vector", lambda e: e.tensor_copy(out=mu1c[:, 0:NT], in_=T256a[:, 0:NT]))
        S.drain("vector")
        S.op("vector", lambda e: e.tensor_scalar(
            out=r1c[:, 0:NT], in0=T256b[:, 0:NT], scalar1=1e-5, scalar2=None,
            op0=ALU.add), "dv")
        dv_r1pre = S.cnt("dv")
        S.wait("scalar", "dv", dv_r1pre)
        S.op("scalar", lambda e: e.activation(out=r1c[:, 0:NT], in_=r1c[:, 0:NT],
                                              func=AF.Sqrt), "sc")
        S.wait("vector", "sc", S.cnt("sc"))
        S.drain("vector")
        S.op("vector", lambda e: e.reciprocal(out=r1c[:, 0:NT], in_=r1c[:, 0:NT]))
        S.drain("vector")
        S.op("vector", lambda e: e.tensor_scalar(
            out=r2c[:, 0:NT], in0=T256b[:, 0:NT], scalar1=1e-5, scalar2=None,
            op0=ALU.add))
        S.drain("vector")
        S.op("vector", lambda e: e.reciprocal(out=r2c[:, 0:NT], in_=r2c[:, 0:NT]))
        S.drain("vector")
        S.op("vector", lambda e: e.tensor_tensor(
            out=r2c[:, 0:NT], in0=r2c[:, 0:NT], in1=T256b[:, 0:NT], op=ALU.mult))
        S.drain("vector")
        S.op("vector", lambda e: e.tensor_scalar(
            out=r2c[:, 0:NT], in0=r2c[:, 0:NT], scalar1=1e-5, scalar2=None,
            op0=ALU.add), "dv")
        S.wait("scalar", "dv", S.cnt("dv"))
        S.op("scalar", lambda e: e.activation(out=r2c[:, 0:NT], in_=r2c[:, 0:NT],
                                              func=AF.Sqrt), "sc")
        S.wait("vector", "sc", S.cnt("sc"))
        S.drain("vector")
        S.op("vector", lambda e: e.reciprocal(out=r2c[:, 0:NT], in_=r2c[:, 0:NT]))
        S.drain("vector")
        S.op("vector", lambda e: e.tensor_tensor(
            out=r12c[:, 0:NT], in0=r1c[:, 0:NT], in1=r2c[:, 0:NT], op=ALU.mult),
            "dv")
        dv_stats = S.cnt("dv")
        S.drain("vector")
        S.op("vector", lambda e: e.tensor_scalar(
            out=T256b[:, 0:NW], in0=T256b[:, 0:NW], scalar1=1e-5, scalar2=None,
            op0=ALU.add))
        S.drain("vector")
        S.op("vector", lambda e: e.reciprocal(out=T256b[:, 0:NW], in_=T256b[:, 0:NW]))
        S.drain("vector")
        S.op("vector", lambda e: e.tensor_tensor(
            out=st_mx[:, 0:NW], in0=st_mx[:, 0:NW], in1=T256b[:, 0:NW], op=ALU.mult))
        S.drain("vector")
        S.op("vector", lambda e: e.reduce_max(out=tcol[:, 2:3], in_=st_mx[:, 0:NW],
                                              axis=AX.X), "dv")
        S.wait("tensor", "dv", S.cnt("dv"))
        S.op("tensor", lambda e: e.transpose(p6[0:1, 0:128], tcol[:, 2:3], id32[:]),
             "pe")
        S.wait("vector", "pe", S.cnt("pe"))
        S.drain("vector")
        S.op("vector", lambda e: e.tensor_copy(out=trow[:], in_=p6[0:1, 0:128]))
        S.drain("vector")
        S.op("vector", lambda e: e.reduce_max(out=sca[0:1, 4:5], in_=trow[:],
                                              axis=AX.X), "dv")
        S.wait("scalar", "dv", S.cnt("dv"))
        S.op("scalar", lambda e: e.activation(out=sca[0:1, 4:5], in_=sca[0:1, 4:5],
                                              func=AF.Sqrt), "sc")
        S.wait("vector", "sc", S.cnt("sc"))
        S.drain("vector")
        S.op("vector", lambda e: e.tensor_scalar(
            out=sca[0:1, 4:5], in0=sca[0:1, 4:5], scalar1=1.0 / 1024.0,
            scalar2=None, op0=ALU.mult))
        S.drain("vector")
        S.op("vector", lambda e: e.tensor_scalar(
            out=sca[0:1, 5:6], in0=sca[0:1, 4:5], scalar1=1e-12, scalar2=None,
            op0=ALU.max))
        S.drain("vector")
        S.op("vector", lambda e: e.reciprocal(out=sca[0:1, 6:7], in_=sca[0:1, 5:6]))
        S.drain("vector")
        S.op("vector", lambda e: e.tensor_scalar(
            out=sca[0:1, 6:7], in0=sca[0:1, 6:7], scalar1=224.0, scalar2=None,
            op0=ALU.mult))
        S.drain("vector")
        S.op("vector", lambda e: e.tensor_tensor(
            out=sca[0:1, 7:8], in0=sca[0:1, 5:6], in1=sca[0:1, 2:3], op=ALU.mult))
        S.drain("vector")
        S.op("vector", lambda e: e.tensor_scalar(
            out=sca[0:1, 7:8], in0=sca[0:1, 7:8], scalar1=1.0 / (224.0 * 224.0),
            scalar2=None, op0=ALU.mult), "dv")
        S.wait("tensor", "dv", S.cnt("dv"))
        S.op("tensor", lambda e: e.matmul(p6[:, 0:1], onef[:], sca[0:1, 6:7],
                                          start=True, stop=True))
        S.op("tensor", lambda e: e.matmul(p6[:, 1:2], onef[:], sca[0:1, 7:8],
                                          start=True, stop=True), "pe")
        S.wait("vector", "pe", S.cnt("pe"))
        S.drain("vector")
        S.op("vector", lambda e: e.tensor_copy(out=sxcol[:], in_=p6[:, 0:1]))
        S.drain("vector")
        S.op("vector", lambda e: e.tensor_copy(out=ccol[:], in_=p6[:, 1:2]))
        S.drain("vector")
        S.op("vector", lambda e: e.tensor_scalar(
            out=D1[:], in0=id32[:], scalar1=sxcol[:], scalar2=None,
            op0=ALU.mult))
        S.drain("vector")
        S.op("vector", lambda e: e.reciprocal(out=sca[0:1, 0:1], in_=sca[0:1, 7:8]))
        S.wait("vector", "gl", gl_bf)
        S.drain("vector")
        S.op("vector", lambda e: e.tensor_scalar(
            out=bfrb[:], in0=bfrb[:], scalar1=sca[0:1, 0:1], scalar2=None,
            op0=ALU.mult), "dv")
        dv_scales = S.cnt("dv")

        if debug_stop == "passA":
            S.wait("sync", "dv", S.cnt("dv"))
            S.wait("sync", "sc", S.cnt("sc"))
            S.wait("sync", "pe", S.cnt("pe"))
            for i in range(T // 128):
                S.wait("sync", "sta", 16 * i)
                S.op("sync", lambda e, i=i: e.dma_start(
                    out=y_out[i * 128:(i + 1) * 128, :], in_=XN[:, 0:1024]),
                    "sta", 16)
            S.wait("sync", "sta", S.cnt("sta"))
            _emit_all()
            return nc

        S.wait("sync", "dv", dv_scales)
        S.wait("vector", "sc", S.cnt("sc"))
        LX = ["lxa", "lxb"]
        for sub in range(2):
            S.op("sync", lambda e, sub=sub: e.dma_start(
                out=XIN[0][:, sub * 1024:(sub + 1) * 1024],
                in_=x_own[sub * 128:(sub + 1) * 128, :]), "lxa", 16)

        bdv = {}
        bsc = {}
        hq_ready = [0] * 8
        def bank_wait(*banks):
            for b in banks:
                v = bdv.get(id(b), 0)
                if v:
                    S.wait("tensor", "dv", v)
                v = bsc.get(id(b), 0)
                if v:
                    S.wait("tensor", "sc", v)
        st_cnt = 0
        for m in range(NMAC):
            xin = XIN[m % 2]
            if m + 1 < NMAC:
                if m == 0:
                    S.wait("sync", "dv", dv_wfq)
                S.wait("sync", "gp", S.cnt("gp"))
                for sub in range(2):
                    S.op("sync", lambda e, m=m, sub=sub: e.dma_start(
                        out=XIN[(m + 1) % 2][:, sub * 1024:(sub + 1) * 1024],
                        in_=x_own[(m + 1) * 256 + sub * 128:
                                  (m + 1) * 256 + (sub + 1) * 128, :]),
                        LX[(m + 1) % 2], 16)

            if m == 0:
                xn_done = [0, 0]
                S.wait("vector", LX[0], 32)
                for sub in range(2):
                    S.op("vector", lambda e, sub=sub: e.tensor_scalar(
                        out=XN[:, sub * 1024:(sub + 1) * 1024],
                        in0=XIN[0][:, sub * 1024:(sub + 1) * 1024],
                        scalar1=mu1c[:, sub:sub + 1], scalar2=r1c[:, sub:sub + 1],
                        op0=ALU.subtract, op1=ALU.mult))
                    S.op("vector", lambda e, sub=sub: e.tensor_scalar(
                        out=D2T[:, sub * 128:(sub + 1) * 128], in0=id32[:],
                        scalar1=r2c[:, sub:sub + 1], scalar2=None,
                        op0=ALU.mult), "dv")
                    xn_done[sub] = S.cnt("dv")

            tb_rot = (p0, p1, p6, p7)
            tbi = 0
            sc_zt = 0
            for sub in range(2):
                S.wait("tensor", "dv", xn_done[sub])
                for part in ("z", "x"):
                    dst = ZT if part == "z" else XQ
                    rhs = (D2T[:, sub * 128:(sub + 1) * 128] if part == "z"
                           else D1[:])
                    for half in range(2):
                        bank = tb_rot[tbi % 4]
                        tbi += 1
                        bank_wait(bank)
                        pe_t = None
                        for v in range(4):
                            kk = half * 4 + v
                            pe_t = S.op(
                                "tensor", lambda e, bank=bank, v=v, sub=sub,
                                kk=kk, rhs=rhs:
                                e.matmul(bank[:, v * 128:(v + 1) * 128],
                                         XN[:, sub * 1024 + kk * 128:
                                            sub * 1024 + (kk + 1) * 128],
                                         rhs, start=True, stop=True),
                                "pe" if v == 3 else None)
                        S.wait("scalar", "pe", pe_t)
                        for v in range(4):
                            kk = half * 4 + v
                            S.op("scalar", lambda e, bank=bank, v=v, sub=sub,
                                 kk=kk, dst=dst:
                                 e.copy(out=dst[:, kk * 256 + sub * 128:
                                                kk * 256 + (sub + 1) * 128],
                                        in_=bank[:, v * 128:(v + 1) * 128]),
                                 "sc")
                        bsc[id(bank)] = S.cnt("sc")
                    if part == "z" and sub == 1:
                        sc_zt = S.cnt("sc")

            if m == 0 and debug_stop == "ph2":
                S.wait("sync", "dv", S.cnt("dv"))
                S.wait("sync", "sc", S.cnt("sc"))
                S.wait("sync", "pe", S.cnt("pe"))
                S.wait("sync", "gp", S.cnt("gp"))
                for i in range(T // 128):
                    S.wait("sync", "stb", 16 * i)
                    S.op("sync", lambda e, i=i: e.dma_start(
                        out=y_out[i * 128:(i + 1) * 128, :], in_=XN[:, 0:1024]),
                        "stb", 16)
                S.wait("sync", "stb", S.cnt("stb"))
                _emit_all()
                return nc

            S.wait("tensor", "sc", sc_zt)
            if m == 0:
                S.wait("tensor", "dv", dv_wfq)
            for proj, Wn, brow_bf, Mden in (("q", "q", bqr, Mq2),
                                            ("k", "k", bkr, Mk2),
                                            ("g", "g", bgr, None)):
                for g in range(8):
                    bank = (p0, p1, p6, p7)[g % 4]
                    view = bank[:, (g % 8 // 4) * 256:(g % 8 // 4) * 256 + 256]
                    bank_wait(bank)
                    for kk in range(8):
                        S.op("tensor", lambda e, Wn=Wn, g=g, kk=kk, view=view:
                             e.matmul(view,
                                      Wb[Wn][:, kk * 1024 + g * 128:
                                             kk * 1024 + (g + 1) * 128],
                                      ZT[:, kk * 256:(kk + 1) * 256],
                                      start=(kk == 0), stop=False))
                    pe_g = S.op("tensor", lambda e, g=g, view=view, brow_bf=brow_bf:
                                e.matmul(view, brow_bf[0:1, g * 128:(g + 1) * 128],
                                         onebf[0:1, 0:256],
                                         start=False, stop=True), "pe")
                    if proj == "g":
                        S.wait("scalar", "pe", pe_g)
                        S.op("scalar", lambda e, g=g, view=view:
                             e.activation(out=GT[:, g * 256:(g + 1) * 256],
                                          in_=view, func=AF.Silu), "sc")
                        bsc[id(bank)] = S.cnt("sc")
                    elif proj == "q":
                        S.wait("vector", "pe", pe_g)
                        for c in range(2):
                            S.op("vector", lambda e, g=g, c=c, view=view, Mden=Mden:
                                 e.tensor_tensor(
                                     out=Q2[:, g * 256 + c * 128:
                                            g * 256 + (c + 1) * 128],
                                     in0=view[:, c * 128:(c + 1) * 128],
                                     in1=Mden[:, g * 128:(g + 1) * 128],
                                     op=ALU.mult), "dv")
                        bdv[id(bank)] = S.cnt("dv")
                    else:
                        S.wait("vector", "pe", pe_g)
                        for c in range(2):
                            R = (c * 8 + g) * 256
                            for ha in range(2):
                                ho = ha * 64
                                S.op("vector", lambda e, g=g, c=c, ha=ha, ho=ho,
                                     R=R, view=view, Mden=Mden:
                                     e.tensor_tensor(
                                         out=K2BD[ho:ho + 64,
                                                  R + ha * 128:R + (ha + 1) * 128],
                                         in0=view[ho:ho + 64,
                                                  c * 128:(c + 1) * 128],
                                         in1=Mden[ho:ho + 64,
                                                  g * 128:(g + 1) * 128],
                                         op=ALU.mult))
                            S.op("vector", lambda e, g=g, c=c, view=view:
                                 e.tensor_tensor(
                                     out=K3F[:, g * 256 + c * 128:
                                             g * 256 + (c + 1) * 128],
                                     in0=view[:, c * 128:(c + 1) * 128],
                                     in1=Mk3[:, g * 128:(g + 1) * 128],
                                     op=ALU.mult), "dv")
                        bdv[id(bank)] = S.cnt("dv")
            dv_qkg = S.cnt("dv")
            sc_g = S.cnt("sc")

            S.wait("tensor", "dv", dv_qkg)
            for batch in range(2):
                bank = (pta, ptb)[batch]
                bank_wait(bank)
                pe_t = None
                for v in range(8):
                    idx = batch * 8 + v
                    g, c = idx // 2, idx % 2
                    pe_t = S.op("tensor", lambda e, bank=bank, v=v, g=g, c=c:
                         e.transpose(bank[:, v * 128:(v + 1) * 128],
                                     K3F[:, g * 256 + c * 128:g * 256 + (c + 1) * 128],
                                     idbf[:]),
                         "pe" if v == 7 else None)
                S.wait("scalar", "pe", pe_t)
                for v in range(8):
                    idx = batch * 8 + v
                    g, c = idx // 2, idx % 2
                    S.op("scalar", lambda e, bank=bank, v=v, g=g, c=c:
                         e.copy(
                             out=K3T[:, (c * 8 + g) * 128:(c * 8 + g + 1) * 128],
                             in_=bank[:, v * 128:(v + 1) * 128]), "sc")
                bsc[id(bank)] = S.cnt("sc")

            for sub in range(2):
                bank_wait(p4, p5)
                for half, bank in ((0, p4), (1, p5)):
                    for kk in range(8):
                        S.op("tensor", lambda e, sub=sub, kk=kk, half=half, bank=bank:
                             e.matmul(bank[:, 0:512],
                                      ZT[:, kk * 256 + sub * 128:
                                         kk * 256 + (sub + 1) * 128],
                                      Wb["v"][:, kk * 1024 + half * 512:
                                              kk * 1024 + (half + 1) * 512],
                                      start=(kk == 0), stop=False))
                    pe_v = S.op("tensor", lambda e, half=half, bank=bank: e.matmul(
                        bank[:, 0:512], onebf[0:1, 0:128],
                        bvb[0:1, half * 512:(half + 1) * 512],
                        start=False, stop=True),
                        "pe" if half == 1 else None)
                S.wait("scalar", "pe", pe_v)
                for half, bank in ((0, p4), (1, p5)):
                    S.op("scalar", lambda e, sub=sub, half=half, bank=bank:
                         e.copy(
                             out=VT[:, sub * 1024 + half * 512:
                                    sub * 1024 + (half + 1) * 512],
                             in_=bank[:, 0:512]), "sc")
                bsc[id(p4)] = bsc[id(p5)] = S.cnt("sc")
            dv_v = S.cnt("dv")
            sc_v = S.cnt("sc")

            if m == 0 and debug_stop == "ph3":
                S.wait("sync", "dv", S.cnt("dv"))
                S.wait("sync", "sc", S.cnt("sc"))
                S.wait("sync", "pe", S.cnt("pe"))
                S.wait("sync", "gp", S.cnt("gp"))
                for i in range(T // 128):
                    S.wait("sync", "stb", 16 * i)
                    S.op("sync", lambda e, i=i: e.dma_start(
                        out=y_out[i * 128:(i + 1) * 128, :], in_=XN[:, 0:1024]),
                        "stb", 16)
                S.wait("sync", "stb", S.cnt("stb"))
                _emit_all()
                return nc

            S.wait("tensor", "dv", dv_v)
            S.wait("tensor", "sc", sc_v)
            if m == 0:
                S.wait("tensor", "gp", gp_init)
            units = [(c, g) for c in range(2) for g in range(8)]
            dv_am_u = [0] * 16

            def emit_A(idx):
                c, g = units[idx]
                pa_bank = (p6, p7)[idx % 2]
                amsl = (idx % 2) * 256
                qcol = g * 256 + c * 128
                bank_wait(pa_bank)
                pe_a = None
                R = (c * 8 + g) * 256
                for ha in range(2):
                    ac = ha * 128
                    S.op("tensor", lambda e, R=R, ha=ha, ac=ac,
                         pa_bank=pa_bank, qcol=qcol: e.matmul(
                             pa_bank[0:64, ac:ac + 128],
                             K2BD[:, R + ha * 128:R + ha * 128 + 64],
                             Q2[:, qcol:qcol + 128],
                             start=True, stop=True))
                    pe_a = S.op("tensor", lambda e, R=R, ha=ha, ac=ac,
                         pa_bank=pa_bank, qcol=qcol: e.matmul(
                             pa_bank[64:128, ac + 64:ac + 128],
                             K2BD[:, R + ha * 128 + 64:R + ha * 128 + 128],
                             Q2[:, qcol + 64:qcol + 128],
                             start=True, stop=True),
                         "pe" if ha == 1 else None)
                S.wait("vector", "pe", pe_a)
                for ha in range(2):
                    ac = ha * 128
                    S.op("vector", lambda e, pa_bank=pa_bank, ac=ac, amsl=amsl:
                         e.tensor_tensor(
                             out=AM[0:64, amsl + ac:amsl + ac + 128],
                             in0=pa_bank[0:64, ac:ac + 128],
                             in1=mUT[0:64, 0:128], op=ALU.mult))
                    S.op("vector", lambda e, pa_bank=pa_bank, ac=ac, amsl=amsl:
                         e.tensor_tensor(
                             out=AM[64:128, amsl + ac + 64:amsl + ac + 128],
                             in0=pa_bank[64:128, ac + 64:ac + 128],
                             in1=mUT[64:128, 64:128], op=ALU.mult), "dv")
                dv_am_u[idx] = S.cnt("dv")
                bdv[id(pa_bank)] = dv_am_u[idx]

            def emit_oU(idx):
                c, g = units[idx]
                po_bank = (p0, p1)[idx % 2]
                pu_bank = (p4, p5)[idx % 2]
                amsl = (idx % 2) * 256
                qcol = g * 256 + c * 128
                bank_wait(po_bank, pu_bank)
                if hq_ready[g]:
                    S.wait("tensor", "dv", hq_ready[g])
                S.wait("tensor", "dv", dv_am_u[idx])
                for ha in range(2):
                    h = 2 * g + ha
                    S.op("tensor", lambda e, g=g, ha=ha, qcol=qcol,
                         po_bank=po_bank: e.matmul(
                             po_bank[ha * 64:ha * 64 + 64, 0:128],
                             HQB[:, g * 128 + ha * 64:g * 128 + (ha + 1) * 64],
                             Q2[:, qcol:qcol + 128], start=True, stop=False))
                    S.op("tensor", lambda e, c=c, h=h, ha=ha, amsl=amsl,
                         po_bank=po_bank: e.matmul(
                             po_bank[ha * 64:ha * 64 + 64, 0:128],
                             VT[:, c * 1024 + h * 64:c * 1024 + (h + 1) * 64],
                             AM[:, amsl + ha * 128:amsl + (ha + 1) * 128],
                             start=False, stop=True))
                pe_ou = S.op("tensor", lambda e, g=g, c=c, pu_bank=pu_bank:
                             e.matmul(
                                 pu_bank[:, 0:128],
                                 K3T[:, (c * 8 + g) * 128:(c * 8 + g + 1) * 128],
                                 VT[:, c * 1024 + g * 128:c * 1024 + (g + 1) * 128],
                                 start=True, stop=True), "pe")
                S.wait("vector", "pe", pe_ou)
                S.op("vector", lambda e, po_bank=po_bank, qcol=qcol:
                     e.tensor_tensor(
                         out=OG[:, qcol:qcol + 128], in0=po_bank[:, 0:128],
                         in1=GT[:, qcol:qcol + 128], op=ALU.mult), "dv")
                bdv[id(po_bank)] = S.cnt("dv")
                for ha in range(2):
                    ho = ha * 64
                    S.op("vector", lambda e, g=g, ho=ho,
                         pu_bank=pu_bank:
                         e.scalar_tensor_tensor(
                             out=HQB[ho:ho + 64, g * 128 + ho:g * 128 + ho + 64],
                             in0=HQB[ho:ho + 64, g * 128 + ho:g * 128 + ho + 64],
                             scalar=a128c[ho:ho + 64, g:g + 1],
                             in1=pu_bank[ho:ho + 64, ho:ho + 64],
                             op0=ALU.mult, op1=ALU.add),
                         "dv" if ha == 1 else None)
                bdv[id(pu_bank)] = S.cnt("dv")
                hq_ready[g] = S.cnt("dv")

            for idx in range(16):
                emit_A(idx)
                if idx >= 1:
                    emit_oU(idx - 1)
            emit_oU(15)
            dv_rec = S.cnt("dv")

            if m == 0 and debug_stop == "ph4":
                S.wait("sync", "dv", S.cnt("dv"))
                S.wait("sync", "sc", S.cnt("sc"))
                S.wait("sync", "pe", S.cnt("pe"))
                S.wait("sync", "gp", S.cnt("gp"))
                for i in range(T // 128):
                    S.wait("sync", "stb", 16 * i)
                    S.op("sync", lambda e, i=i: e.dma_start(
                        out=y_out[i * 128:(i + 1) * 128, :], in_=XN[:, 0:1024]),
                        "stb", 16)
                S.wait("sync", "stb", S.cnt("stb"))
                _emit_all()
                return nc

            pe_y = [0, 0]
            pe_f = [0, 0]
            sc_si = [0, 0]
            nxt = m + 1 < NMAC

            def ymms(sub):
                for half, bank in ((0, p4), (1, p5)):
                    for kk in range(8):
                        S.op("tensor", lambda e, sub=sub, kk=kk, half=half,
                             bank=bank:
                             e.matmul(bank[:, 0:512],
                                      OG[:, kk * 256 + sub * 128:
                                         kk * 256 + (sub + 1) * 128],
                                      Wb["o"][:, kk * 1024 + half * 512:
                                              kk * 1024 + (half + 1) * 512],
                                      start=(kk == 0), stop=False))
                    pe_y[sub] = S.op(
                        "tensor", lambda e, half=half, bank=bank:
                        e.matmul(bank[:, 0:512], onebf[0:1, 0:128],
                                 bob[0:1, half * 512:(half + 1) * 512],
                                 start=False, stop=True),
                        "pe" if half == 1 else None)

            def fmms(sub, banks):
                for half, bank in ((0, banks[0]), (1, banks[1])):
                    for kk in range(8):
                        S.op("tensor", lambda e, sub=sub, kk=kk, half=half,
                             bank=bank:
                             e.matmul(bank[:, 0:512],
                                      XQ[:, kk * 256 + sub * 128:
                                         kk * 256 + (sub + 1) * 128],
                                      WFc[:, kk * 1024 + half * 512:
                                          kk * 1024 + (half + 1) * 512],
                                      start=(kk == 0), stop=False))
                    pe_f[sub] = S.op(
                        "tensor", lambda e, half=half, bank=bank:
                        e.matmul(bank[:, 0:512], onebf[0:1, 0:128],
                                 bfrb[0:1, half * 512:(half + 1) * 512],
                                 start=False, stop=True),
                        "pe" if half == 1 else None)

            def outadd(sub):
                S.wait("vector", "pe", pe_y[sub])
                S.wait("vector", ("sta", "stb")[sub], 16 * m)
                for half, bank in ((0, p4), (1, p5)):
                    S.op("vector", lambda e, sub=sub, half=half, bank=bank:
                         e.tensor_tensor(
                             out=OUT[:, sub * 1024 + half * 512:
                                    sub * 1024 + (half + 1) * 512],
                             in0=bank[:, 0:512],
                             in1=XN[:, sub * 1024 + half * 512:
                                    sub * 1024 + (half + 1) * 512],
                             op=ALU.add), "dv")
                bdv[id(p4)] = bdv[id(p5)] = S.cnt("dv")
                return S.cnt("dv")

            def fevac(sub, banks):
                S.wait("vector", "pe", pe_f[sub])
                for half, bank in ((0, banks[0]), (1, banks[1])):
                    S.op("vector", lambda e, sub=sub, half=half, bank=bank:
                         e.tensor_scalar(
                             out=FB[:, sub * 1024 + half * 512:
                                    sub * 1024 + (half + 1) * 512],
                             in0=bank[:, 0:512], scalar1=ccol[:],
                             scalar2=None, op0=ALU.mult), "dv")
                bdv[id(banks[0])] = bdv[id(banks[1])] = S.cnt("dv")
                S.wait("scalar", "dv", S.cnt("dv"))
                sc_si[sub] = S.op("scalar", lambda e, sub=sub: e.activation(
                    out=FB[:, sub * 1024:(sub + 1) * 1024],
                    in_=FB[:, sub * 1024:(sub + 1) * 1024], func=AF.Silu), "sc")

            S.wait("tensor", "dv", dv_rec)
            bank_wait(p4, p5)
            ymms(0)
            bank_wait(p6, p7)
            fmms(0, (p6, p7))
            dv_oa = [0, 0]
            dv_oa[0] = outadd(0)
            if nxt:
                xn_done = [0, 0]
                S.wait("vector", LX[(m + 1) % 2], 32 * ((m + 1) // 2 + 1))
                S.drain("vector")
                ti = (m + 1) * 2
                S.op("vector", lambda e, ti=ti, xn_in=XIN[(m + 1) % 2]:
                     e.tensor_scalar(
                         out=XN[:, 0:1024], in0=xn_in[:, 0:1024],
                         scalar1=mu1c[:, ti:ti + 1], scalar2=r1c[:, ti:ti + 1],
                         op0=ALU.subtract, op1=ALU.mult))
                S.op("vector", lambda e, ti=ti: e.tensor_scalar(
                    out=D2T[:, 0:128], in0=id32[:],
                    scalar1=r2c[:, ti:ti + 1], scalar2=None, op0=ALU.mult),
                    "dv")
                xn_done[0] = S.cnt("dv")
            fevac(0, (p6, p7))
            bank_wait(p4, p5)
            ymms(1)
            dv_oa[1] = outadd(1)
            if nxt:
                ti = (m + 1) * 2 + 1
                S.drain("vector")
                S.op("vector", lambda e, ti=ti, xn_in=XIN[(m + 1) % 2]:
                     e.tensor_scalar(
                         out=XN[:, 1024:2048], in0=xn_in[:, 1024:2048],
                         scalar1=mu1c[:, ti:ti + 1], scalar2=r1c[:, ti:ti + 1],
                         op0=ALU.subtract, op1=ALU.mult))
                S.op("vector", lambda e, ti=ti: e.tensor_scalar(
                    out=D2T[:, 128:256], in0=id32[:],
                    scalar1=r2c[:, ti:ti + 1], scalar2=None, op0=ALU.mult),
                    "dv")
                xn_done[1] = S.cnt("dv")
            bank_wait(p4, p5)
            fmms(1, (p4, p5))
            fevac(1, (p4, p5))
            S.drain("vector")
            for sub in range(2):
                S.wait("vector", "sc", sc_si[sub])
                dv_sg = S.op("vector", lambda e, sub=sub: e.tensor_tensor(
                    out=OUT[:, sub * 1024:(sub + 1) * 1024],
                    in0=OUT[:, sub * 1024:(sub + 1) * 1024],
                    in1=FB[:, sub * 1024:(sub + 1) * 1024], op=ALU.mult), "dv")
                S.wait("gpsimd", "dv", dv_sg)
                gp_o = S.op("gpsimd", lambda e, sub=sub, xin=xin: e.tensor_tensor(
                    out=OUT[:, sub * 1024:(sub + 1) * 1024],
                    in0=OUT[:, sub * 1024:(sub + 1) * 1024],
                    in1=xin[:, sub * 1024:(sub + 1) * 1024], op=ALU.add), "gp")
                S.wait("sync", "gp", gp_o)
                S.op("sync", lambda e, m=m, sub=sub: e.dma_start(
                    out=y_out[m * 256 + sub * 128:m * 256 + (sub + 1) * 128, :],
                    in_=OUT[:, sub * 1024:(sub + 1) * 1024]),
                    ("sta", "stb")[sub], 16)
                st_cnt += 1

        S.wait("sync", "sta", S.cnt("sta"))
        S.wait("sync", "stb", S.cnt("stb"))

        _emit_all()

    return nc


def _host_inputs(inputs, T=4096):
    x = np.asarray(inputs["L0_attn_x"], dtype=np.float32)
    B = x.shape[0]
    tp = lambda a: np.ascontiguousarray(np.asarray(a, np.float32).T)
    col8 = lambda v: np.ascontiguousarray(
        np.asarray(v, np.float32).reshape(8, 128).T)
    wts = {"q": tp(inputs["L0_attn_WQ"]), "k": tp(inputs["L0_attn_WK"]),
           "v": tp(inputs["L0_attn_WV"]), "g": tp(inputs["L0_attn_Wg"]),
           "o": tp(inputs["L0_attn_WO"]), "f": tp(inputs["L0_ffn_weight"])}
    tt = np.arange(128, dtype=np.float32)
    e_q = np.tile(tt - 63.0, (128, 1)).astype(np.float32)
    e_k = np.tile(63.0 - tt, (128, 1)).astype(np.float32)
    e_k3 = np.tile(191.0 - tt, (128, 1)).astype(np.float32)
    maskut = (np.arange(128)[:, None] <= np.arange(128)[None, :]).astype(np.float32)
    common = {
        "wt_q": wts["q"], "wt_k": wts["k"], "wt_v": wts["v"], "wt_g": wts["g"],
        "wt_o": wts["o"], "wt_f": wts["f"],
        "b_q_c": col8(inputs["L0_attn_bQ"]),
        "b_k_c": col8(inputs["L0_attn_bK"]),
        "b_g_c": col8(inputs["L0_attn_bg"]),
        "b_v_r": np.asarray(inputs["L0_attn_bV"], np.float32).reshape(1, D),
        "b_o_r": np.asarray(inputs["L0_attn_bO"], np.float32).reshape(1, D),
        "b_f_r": np.asarray(inputs["L0_ffn_bias"], np.float32).reshape(1, D),
        "gamma_c": col8(inputs["L0_attn_ln_gamma"]),
        "beta_c": col8(inputs["L0_attn_ln_beta"]),
        "alpha2h": np.ascontiguousarray(
            np.asarray(inputs["L0_attn_alpha_logit"], np.float32).reshape(8, 128).T),
        "ident32": np.eye(128, dtype=np.float32),
        "identbf": np.eye(128).astype(ml_dtypes.bfloat16),
        "e_q": e_q, "e_k": e_k, "e_k3": e_k3, "maskut": maskut,
        "ones_bf": np.ones((1, 256), dtype=ml_dtypes.bfloat16),
        "ones_f32": np.ones((1, 128), dtype=np.float32),
    }
    in_maps = []
    for c in range(B):
        rest = np.ascontiguousarray(
            np.concatenate([x[b] for b in range(B) if b != c], axis=0))
        m = dict(common)
        m["x_own"] = np.ascontiguousarray(x[c])
        m["x_rest"] = rest
        in_maps.append(m)
    return in_maps


def kernel(**inputs):
    T = 4096
    in_maps = _host_inputs(inputs, T)
    nc = build_core(T=T, NB_REST=7)
    res = run_bass_kernel_spmd(nc, in_maps, core_ids=list(range(8)))
    out = np.stack([r["y_out"] for r in res.results], axis=0)
    return out.astype(np.float32)


# version 16 (final)
# speedup vs baseline: 1.0105x; 1.0105x over previous
import numpy as np
import ml_dtypes
from contextlib import ExitStack

import concourse.bass as bass
import concourse.mybir as mybir
from concourse.bass_utils import run_bass_kernel_spmd

F32 = mybir.dt.float32
BF16 = mybir.dt.bfloat16
FP8 = mybir.dt.float8e4
AF = mybir.ActivationFunctionType
ALU = mybir.AluOpType
AX = mybir.AxisListType

D = 1024


class Sched:

    ENGS = ("sync", "gpsimd", "vector", "scalar", "tensor")

    def __init__(self):
        self.ops = {e: [] for e in self.ENGS}
        self.count = {}

    def op(self, eng, fn, inc=None, amt=1):
        self.ops[eng].append(("op", fn, inc, amt))
        if inc is not None:
            self.count[inc] = self.count.get(inc, 0) + amt
            return self.count[inc]
        return None

    def drain(self, eng):
        self.ops[eng].append(("op", lambda e: e.drain(), None, 1))

    def wait(self, eng, sem_name, val):
        if val > 0:
            self.ops[eng].append(("wait", sem_name, val))

    def cnt(self, s):
        return self.count.get(s, 0)

    def emit(self, eng_obj, eng_name, sems):
        for item in self.ops[eng_name]:
            if item[0] == "wait":
                eng_obj.wait_ge(sems[item[1]], item[2])
            else:
                _, fn, inc, amt = item
                inst = fn(eng_obj)
                if inc is not None:
                    inst.then_inc(sems[inc], amt)


def build_core(T=4096, NB_REST=7, debug_stop=None):
    NT = T // 128
    NMAC = T // 256
    TR = NB_REST * T
    NTR = TR // 128
    NTOT = NT + NTR

    nc = bass.Bass("TRN2", target_bir_lowering=False)
    dram = lambda n, s, dt, k: nc.dram_tensor(n, s, dt, kind=k).ap()
    x_own = dram("x_own", [T, D], F32, "ExternalInput")
    x_rest = dram("x_rest", [TR, D], F32, "ExternalInput")
    wt = {n: dram("wt_" + n, [D, D], F32, "ExternalInput")
          for n in ("q", "k", "v", "g", "o", "f")}
    b_c = {n: dram("b_" + n + "_c", [128, 8], F32, "ExternalInput")
           for n in ("q", "k", "g")}
    b_r = {n: dram("b_" + n + "_r", [1, D], F32, "ExternalInput")
           for n in ("v", "o", "f")}
    gamma_d = dram("gamma_c", [128, 8], F32, "ExternalInput")
    beta_d = dram("beta_c", [128, 8], F32, "ExternalInput")
    alpha_d = dram("alpha2h", [128, 8], F32, "ExternalInput")
    ident32_d = dram("ident32", [128, 128], F32, "ExternalInput")
    identbf_d = dram("identbf", [128, 128], BF16, "ExternalInput")
    e_q_d = dram("e_q", [128, 128], F32, "ExternalInput")
    e_k_d = dram("e_k", [128, 128], F32, "ExternalInput")
    e_k3_d = dram("e_k3", [128, 128], F32, "ExternalInput")
    maskut_d = dram("maskut", [128, 128], F32, "ExternalInput")
    ones_bf_d = dram("ones_bf", [1, 256], BF16, "ExternalInput")
    ones_f32_d = dram("ones_f32", [1, 128], F32, "ExternalInput")
    y_out = dram("y_out", [T, D], F32, "ExternalOutput")

    S = Sched()
    sem_names = ["lc", "lb", "lda0", "lda1", "lda2", "lda3",
                 "lxa", "lxb", "gl", "pe", "dv", "sc", "gp", "sta", "stb",
                 "wfa", "wfb"]

    with ExitStack() as ctx:
        sb = lambda n, s, dt: ctx.enter_context(nc.sbuf_tensor(n, s, dt))
        pst = lambda n, s, dt: ctx.enter_context(nc.psum_tensor(n, s, dt))

        Wb = {n: sb("W" + n, [128, 8192], BF16) for n in ("q", "k", "v", "g", "o")}
        WFc = sb("WFc", [128, 8192], FP8)
        XIN = [sb("XIN0", [128, 2048], F32), sb("XIN1", [128, 2048], F32)]
        XN = sb("XN", [128, 2048], F32)
        D2T = sb("D2T", [128, 256], F32)
        D1 = sb("D1", [128, 128], F32)
        ZT = sb("ZT", [128, 2048], BF16)
        XQ = sb("XQ", [128, 2048], FP8)
        Q2 = sb("Q2", [128, 2048], BF16)
        K2BD = sb("K2BD", [128, 4096], BF16)
        K3F = sb("K3F", [128, 2048], BF16)
        K3T = sb("K3T", [128, 2048], BF16)
        VT = sb("VT", [128, 2048], BF16)
        GT = sb("GT", [128, 2048], BF16)
        AM = sb("AM", [128, 512], BF16)
        OG = sb("OG", [128, 2048], BF16)
        HQB = sb("HQB", [128, 1024], BF16)
        TR1 = [sb("TR1a", [128, 512], F32), sb("TR1b", [128, 512], F32)]
        FB = sb("FB", [128, 2048], F32)
        OUT = sb("OUT", [128, 2048], F32)
        PA = [OUT[:, 0:1024], OUT[:, 1024:2048], FB[:, 0:1024], FB[:, 1024:2048]]
        st_sq1 = sb("st_sq1", [128, 256], F32)
        st_sq2 = sb("st_sq2", [128, 256], F32)
        st_mx = sb("st_mx", [128, 256], F32)
        st_mn = sb("st_mn", [128, 256], F32)
        mu1c = sb("mu1c", [128, 32], F32)
        r1c = sb("r1c", [128, 32], F32)
        r2c = sb("r2c", [128, 32], F32)
        r12c = sb("r12c", [128, 32], F32)
        id32 = sb("id32", [128, 128], F32)
        idbf = sb("idbf", [128, 128], BF16)
        EtQ = sb("EtQ", [128, 128], F32)
        EtK = sb("EtK", [128, 128], F32)
        EtK3 = sb("EtK3", [128, 128], F32)
        mUT = sb("mUT", [128, 128], F32)
        onebf = sb("onebf", [1, 256], BF16)
        onef = sb("onef", [1, 128], F32)
        onecol = sb("onecol", [128, 1], F32)
        gam = sb("gam", [128, 8], F32)
        bet = sb("bet", [128, 8], F32)
        betbf = sb("betbf", [128, 8], BF16)
        alog = sb("alog", [128, 8], F32)
        Mq2 = sb("Mq2", [128, 1024], BF16)
        Mk2 = sb("Mk2", [128, 1024], BF16)
        Mk3 = sb("Mk3", [128, 1024], BF16)
        bqr = sb("bqr", [1, 1024], BF16)
        bkr = sb("bkr", [1, 1024], BF16)
        bgr = sb("bgr", [1, 1024], BF16)
        a64c = sb("a64c", [128, 8], F32)
        a128c = sb("a128c", [128, 8], F32)
        lnac = sb("lnac", [128, 8], F32)
        bqp = sb("bqp", [128, 8], F32)
        bkp = sb("bkp", [128, 8], F32)
        bgp = sb("bgp", [128, 8], F32)
        bvb = sb("bvb", [1, 1024], BF16)
        bob = sb("bob", [1, 1024], BF16)
        bfrb = sb("bfrb", [1, 1024], BF16)
        tcol = sb("tcol", [128, 8], F32)
        trow = sb("trow", [1, 128], F32)
        sca = sb("sca", [1, 8], F32)
        sxcol = sb("sxcol", [128, 1], F32)
        swcol = sb("swcol", [128, 1], F32)
        ccol = sb("ccol", [128, 1], F32)
        ZSL = [XN[:, 0:1024], XN[:, 1024:2048],
               XIN[0][:, 0:1024], XIN[0][:, 1024:2048]]
        brow = FB[0:1, 0:1024]
        T256a = FB[:, 0:256]
        T256b = FB[:, 256:512]

        p0 = pst("p0", [128, 512], F32)
        p1 = pst("p1", [128, 512], F32)
        pta = pst("pta", [128, 1024], BF16)
        ptb = pst("ptb", [128, 1024], BF16)
        p4 = pst("p4", [128, 512], F32)
        p5 = pst("p5", [128, 512], F32)
        p6 = pst("p6", [128, 512], F32)
        p7 = pst("p7", [128, 512], F32)

        sems = {n: ctx.enter_context(nc.semaphore(n)) for n in sem_names}
        block = ctx.enter_context(nc.Block())

        def _emit_all():
            @block.sync
            def _(eng):
                S.emit(eng, "sync", sems)

            @block.gpsimd
            def _(eng):
                S.emit(eng, "gpsimd", sems)

            @block.vector
            def _(eng):
                S.emit(eng, "vector", sems)

            @block.scalar
            def _(eng):
                S.emit(eng, "scalar", sems)

            @block.tensor
            def _(eng):
                S.emit(eng, "tensor", sems)

        LD = lambda dst, src: S.op(
            "sync", lambda e, d=dst, s=src: e.dma_start(out=d, in_=s), "lc", 16)
        LDA = ["lda0", "lda1", "lda2", "lda3"]

        def warm(sem_name=None, val=0):
            if sem_name is not None and val > 0:
                S.wait("tensor", sem_name, val)
            S.op("tensor", lambda e: e.matmul(
                p7[0:64, 448:512], onebf[0:1, 0:64], onebf[0:1, 64:128],
                start=True, stop=True))

        for dst, src in ((id32[:], ident32_d), (idbf[:], identbf_d),
                         (EtQ[:], e_q_d), (EtK[:], e_k_d), (EtK3[:], e_k3_d),
                         (mUT[:], maskut_d), (onebf[:], ones_bf_d),
                         (onef[:], ones_f32_d), (gam[:], gamma_d),
                         (bet[:], beta_d), (alog[:], alpha_d),
                         (bqp[:], b_c["q"]), (bkp[:], b_c["k"]),
                         (bgp[:], b_c["g"])):
            LD(dst, src)
        ld_consts = S.cnt("lc")
        for n in ("q", "k", "v", "g", "o"):
            for kk in range(8):
                S.op("gpsimd", lambda e, n=n, kk=kk: e.dma_start(
                    out=Wb[n][:, kk * 1024:(kk + 1) * 1024],
                    in_=wt[n][kk * 128:(kk + 1) * 128, :]), "gl", 16)
        gl_w = S.cnt("gl")
        S.op("gpsimd", lambda e: e.memset(HQB[:], 0.0))
        S.op("gpsimd", lambda e: e.memset(AM[:], 0.0))
        S.op("gpsimd", lambda e: e.memset(K2BD[:], 0.0))
        S.op("gpsimd", lambda e: e.memset(onecol[:], 1.0), "gp")
        gp_init = S.cnt("gp")

        S.wait("scalar", "lc", ld_consts)
        S.op("scalar", lambda e: e.activation(out=tcol[:], in_=alog[:], func=AF.Sigmoid))
        S.drain("scalar")
        S.op("scalar", lambda e: e.activation(out=lnac[:], in_=tcol[:], func=AF.Ln),
             "sc")
        sc_lna = S.cnt("sc")
        S.wait("vector", "lc", ld_consts)
        S.wait("vector", "sc", sc_lna)
        for g in range(8):
            for j, (M, E) in enumerate(((Mq2, EtQ), (Mk2, EtK), (Mk3, EtK3))):
                half = (g * 3 + j) % 2
                if g * 3 + j >= 2:
                    S.wait("vector", "sc", S.cnt("sc"))
                dvx = S.op("vector", lambda e, g=g, E=E, half=half: e.tensor_scalar(
                    out=TR1[0][:, half * 128:(half + 1) * 128], in0=E[:],
                    scalar1=lnac[:, g:g + 1], scalar2=None, op0=ALU.mult), "dv")
                S.wait("scalar", "dv", dvx)
                S.op("scalar", lambda e, g=g, M=M, half=half: e.activation(
                    out=M[:, g * 128:(g + 1) * 128],
                    in_=TR1[0][:, half * 128:(half + 1) * 128], func=AF.Exp), "sc")
        S.op("vector", lambda e: e.tensor_scalar(
            out=a64c[:], in0=lnac[:], scalar1=64.0, scalar2=None, op0=ALU.mult))
        S.op("vector", lambda e: e.tensor_scalar(
            out=a128c[:], in0=lnac[:], scalar1=128.0, scalar2=None, op0=ALU.mult),
            "dv")
        dv_m = S.cnt("dv")
        S.wait("scalar", "dv", dv_m)
        S.op("scalar", lambda e: e.activation(out=a64c[:], in_=a64c[:], func=AF.Exp))
        S.op("scalar", lambda e: e.activation(out=a128c[:], in_=a128c[:], func=AF.Exp),
             "sc")
        sc_alpha = S.cnt("sc")

        S.op("vector", lambda e: e.tensor_copy(out=betbf[:], in_=bet[:]), "dv")
        dv_bet = S.cnt("dv")

        S.wait("tensor", "gl", gl_w)
        S.wait("tensor", "dv", dv_bet)
        for n, dstc in (("q", bqp), ("k", bkp), ("g", bgp)):
            pe_n = None
            for g in range(8):
                for kk in range(8):
                    last = (g == 7 and kk == 7)
                    pe_n = S.op("tensor", lambda e, n=n, g=g, kk=kk: e.matmul(
                        p0[:, g:g + 1],
                        Wb[n][:, kk * 1024 + g * 128: kk * 1024 + (g + 1) * 128],
                        betbf[:, kk:kk + 1],
                        start=(kk == 0), stop=(kk == 7)),
                        "pe" if last else None)
            S.wait("vector", "pe", pe_n)
            S.op("vector", lambda e, dstc=dstc: e.tensor_tensor(
                out=dstc[:], in0=p0[:, 0:8], in1=dstc[:], op=ALU.add), "dv")
            S.wait("tensor", "dv", S.cnt("dv"))
        for colt, rowt in ((bqp, bqr), (bkp, bkr), (bgp, bgr)):
            for g in range(8):
                S.wait("tensor", "dv", S.cnt("dv"))
                pt = S.op("tensor", lambda e, colt=colt, g=g: e.transpose(
                    p6[0:1, 0:128], colt[:, g:g + 1], id32[:]), "pe")
                S.wait("vector", "pe", pt)
                S.op("vector", lambda e, rowt=rowt, g=g: e.tensor_copy(
                    out=rowt[0:1, g * 128:(g + 1) * 128], in_=p6[0:1, 0:128]), "dv")
        S.op("sync", lambda e: e.dma_start(out=brow, in_=b_r["v"]), "lb", 16)
        ld_bv = S.cnt("lb")
        for half in range(2):
            pe_v = None
            for kk in range(8):
                pe_v = S.op("tensor", lambda e, kk=kk, half=half: e.matmul(
                    p1[0:1, 0:512],
                    betbf[:, kk:kk + 1],
                    Wb["v"][:, kk * 1024 + half * 512: kk * 1024 + (half + 1) * 512],
                    start=(kk == 0), stop=(kk == 7)),
                    "pe" if kk == 7 else None)
            S.wait("vector", "pe", pe_v)
            S.wait("vector", "lb", ld_bv)
            S.op("vector", lambda e, half=half: e.scalar_tensor_tensor(
                out=bvb[0:1, half * 512:(half + 1) * 512], in0=p1[0:1, 0:512],
                scalar=1.0, in1=brow[0:1, half * 512:(half + 1) * 512],
                op0=ALU.mult, op1=ALU.add), "dv")
            S.wait("tensor", "dv", S.cnt("dv"))
        S.wait("sync", "dv", S.cnt("dv"))
        S.wait("sync", "lb", S.cnt("lb"))
        S.op("sync", lambda e: e.dma_start(out=brow, in_=b_r["o"]), "lb", 16)
        ld_bo = S.cnt("lb")
        S.wait("vector", "lb", ld_bo)
        S.op("vector", lambda e: e.tensor_copy(out=bob[:], in_=brow), "dv")
        dv_bob = S.cnt("dv")
        S.wait("gpsimd", "gl", S.cnt("gl"))
        S.op("gpsimd", lambda e: e.dma_start(out=bfrb[:], in_=b_r["f"]), "gl", 16)
        gl_bf = S.cnt("gl")

        fold_jobs = [(n, kk) for n in ("q", "k", "v", "g") for kk in range(8)]

        WFBUF = [XIN[1][:, 0:1024], XIN[1][:, 1024:2048]]
        WFS = ["wfa", "wfb"]
        for kk in range(8):
            buf = WFBUF[kk % 2]
            sld = WFS[kk % 2]
            if kk >= 2:
                S.wait("sync", "dv", S.cnt("dv"))
            S.op("sync", lambda e, kk=kk, buf=buf: e.dma_start(
                out=buf, in_=wt["f"][kk * 128:(kk + 1) * 128, :]), sld, 16)
            warm(sld, S.cnt(sld))
            S.wait("vector", sld, S.cnt(sld))
            if kk == 0:
                S.op("vector", lambda e, buf=buf: e.reduce_max(
                    out=tcol[:, 0:1], in_=buf, axis=AX.X,
                    apply_absolute_value=True), "dv")
            else:
                S.op("vector", lambda e, buf=buf: e.reduce_max(
                    out=tcol[:, 1:2], in_=buf, axis=AX.X,
                    apply_absolute_value=True))
                S.drain("vector")
                S.op("vector", lambda e: e.tensor_tensor(
                    out=tcol[:, 0:1], in0=tcol[:, 0:1], in1=tcol[:, 1:2],
                    op=ALU.max), "dv")
        S.wait("tensor", "dv", S.cnt("dv"))
        S.op("tensor", lambda e: e.transpose(p6[0:1, 0:128], tcol[:, 0:1], id32[:]),
             "pe")
        S.wait("vector", "pe", S.cnt("pe"))
        S.op("vector", lambda e: e.tensor_copy(out=trow[:], in_=p6[0:1, 0:128]))
        S.drain("vector")
        S.op("vector", lambda e: e.reduce_max(out=sca[0:1, 1:2], in_=trow[:], axis=AX.X))
        S.drain("vector")
        S.op("vector", lambda e: e.tensor_scalar(
            out=sca[0:1, 2:3], in0=sca[0:1, 1:2], scalar1=1e-12, scalar2=None,
            op0=ALU.max))
        S.drain("vector")
        S.op("vector", lambda e: e.reciprocal(out=sca[0:1, 3:4], in_=sca[0:1, 2:3]))
        S.drain("vector")
        S.op("vector", lambda e: e.tensor_scalar(
            out=sca[0:1, 3:4], in0=sca[0:1, 3:4], scalar1=224.0, scalar2=None,
            op0=ALU.mult), "dv")
        S.wait("tensor", "dv", S.cnt("dv"))
        S.op("tensor", lambda e: e.matmul(p6[:, 0:1], onef[:], sca[0:1, 3:4],
                                          start=True, stop=True), "pe")
        S.wait("vector", "pe", S.cnt("pe"))
        S.op("vector", lambda e: e.tensor_copy(out=swcol[:], in_=p6[:, 0:1]), "dv")
        S.drain("vector")
        dv_swc = S.cnt("dv")
        for kk in range(8):
            buf = WFBUF[kk % 2]
            sld = WFS[kk % 2]
            S.wait("sync", "dv", S.cnt("dv") if kk >= 2 else dv_swc)
            S.op("sync", lambda e, kk=kk, buf=buf: e.dma_start(
                out=buf, in_=wt["f"][kk * 128:(kk + 1) * 128, :]), sld, 16)
            warm(sld, S.cnt(sld))
            S.wait("vector", sld, S.cnt(sld))
            S.op("vector", lambda e, kk=kk, buf=buf: e.tensor_scalar(
                out=WFc[:, kk * 1024:(kk + 1) * 1024], in0=buf,
                scalar1=swcol[:], scalar2=None, op0=ALU.mult), "dv")
        dv_wfq = S.cnt("dv")

        if debug_stop == "setup":
            S.wait("sync", "dv", S.cnt("dv"))
            S.wait("sync", "sc", S.cnt("sc"))
            S.wait("sync", "pe", S.cnt("pe"))
            S.wait("sync", "gp", S.cnt("gp"))
            for i in range(T // 128):
                S.wait("sync", "sta", 16 * i)
                S.op("sync", lambda e, i=i: e.dma_start(
                    out=y_out[i * 128:(i + 1) * 128, :], in_=XN[:, 0:1024]),
                    "sta", 16)
            S.wait("sync", "sta", S.cnt("sta"))
            _emit_all()
            return nc

        S.wait("scalar", "sc", sc_alpha)
        S.wait("scalar", "gp", gp_init)
        S.wait("gpsimd", "gp", gp_init)
        gp_tree = [0] * NTOT
        dv_red = [0] * NTOT
        sc_z = [0] * NTOT
        dv_max = [0] * NTOT
        for i in range(NTOT):
            buf = PA[i % 4]
            sld = LDA[i % 4]
            if i == 0:
                pass
            if i >= 4:
                S.wait("sync", "gp", gp_tree[i - 4])
                S.wait("sync", "sc", sc_z[i - 4])
            src = (x_own[(i) * 128:(i + 1) * 128, :] if i < NT
                   else x_rest[(i - NT) * 128:(i - NT + 1) * 128, :])
            S.op("sync", lambda e, buf=buf, src=src: e.dma_start(out=buf, in_=src),
                 sld, 16)
            ldv = S.cnt(sld)
            S.wait("gpsimd", sld, ldv)
            if i < 2:
                S.wait("gpsimd", "sc", sc_alpha)
            if i >= 2:
                S.wait("gpsimd", "dv", dv_red[i - 2])
            gp_tree[i] = S.op("gpsimd", lambda e, buf=buf, i=i: e.tensor_tensor(
                out=TR1[i % 2][:], in0=buf[:, 0:512], in1=buf[:, 512:1024],
                op=ALU.add), "gp")
            S.wait("vector", "gp", gp_tree[i])
            dv_red[i] = S.op("vector", lambda e, i=i: e.reduce_sum(
                out=st_sq1[:, i:i + 1], in_=TR1[i % 2][:], axis=AX.X), "dv")
            if i >= 96 and i % 4 == 0 and fold_jobs:
                if i == 96:
                    S.wait("vector", "gl", gl_w)
                n, kk = fold_jobs.pop(0)
                S.op("vector", lambda e, n=n, kk=kk: e.tensor_scalar(
                    out=Wb[n][:, kk * 1024:(kk + 1) * 1024],
                    in0=Wb[n][:, kk * 1024:(kk + 1) * 1024],
                    scalar1=gam[:, kk:kk + 1], scalar2=None, op0=ALU.mult))
            if i >= 2:
                j = i - 2
                S.wait("vector", "sc", sc_z[j])
                dv_max[j] = S.op("vector", lambda e, j=j: e.reduce_max(
                    out=st_mx[:, j:j + 1], in_=ZSL[j % 4], axis=AX.X), "dv")
            S.wait("scalar", "dv", dv_red[i])
            S.wait("scalar", sld, ldv)
            if i >= 4:
                S.wait("scalar", "dv", dv_max[i - 4])
            sc_z[i] = S.op("scalar", lambda e, buf=buf, i=i: e.activation(
                out=ZSL[i % 4], in_=buf, func=AF.Square,
                bias=st_sq1[:, i:i + 1], scale=-1024.0,
                accum_out=st_sq2[:, i:i + 1]), "sc")
            warm(sld, ldv)
        for j in range(NTOT - 2, NTOT):
            S.wait("vector", "sc", sc_z[j])
            dv_max[j] = S.op("vector", lambda e, j=j: e.reduce_max(
                out=st_mx[:, j:j + 1], in_=ZSL[j % 4], axis=AX.X), "dv")
        if fold_jobs:
            S.wait("vector", "gl", gl_w)
            for n, kk in fold_jobs:
                S.op("vector", lambda e, n=n, kk=kk: e.tensor_scalar(
                    out=Wb[n][:, kk * 1024:(kk + 1) * 1024],
                    in0=Wb[n][:, kk * 1024:(kk + 1) * 1024],
                    scalar1=gam[:, kk:kk + 1], scalar2=None, op0=ALU.mult))
            fold_jobs = []
        S.op("vector", lambda e: e.engine_nop(), "dv")
        sc_passA = S.cnt("sc")
        dv_passA = S.cnt("dv")

        NW = NTOT
        S.wait("vector", "sc", sc_passA)
        S.drain("vector")
        S.op("vector", lambda e: e.tensor_scalar(
            out=T256a[:, 0:NW], in0=st_sq1[:, 0:NW], scalar1=1.0 / 1024.0,
            scalar2=None, op0=ALU.mult))
        S.drain("vector")
        S.op("vector", lambda e: e.tensor_scalar(
            out=T256b[:, 0:NW], in0=st_sq2[:, 0:NW],
            scalar1=1.0 / (1024.0 * 1024.0 * 1024.0),
            scalar2=None, op0=ALU.mult))
        S.drain("vector")
        S.op("vector", lambda e: e.tensor_copy(out=mu1c[:, 0:NT], in_=T256a[:, 0:NT]))
        S.drain("vector")
        S.op("vector", lambda e: e.tensor_scalar(
            out=r1c[:, 0:NT], in0=T256b[:, 0:NT], scalar1=1e-5, scalar2=None,
            op0=ALU.add), "dv")
        dv_r1pre = S.cnt("dv")
        S.wait("scalar", "dv", dv_r1pre)
        S.op("scalar", lambda e: e.activation(out=r1c[:, 0:NT], in_=r1c[:, 0:NT],
                                              func=AF.Sqrt), "sc")
        S.wait("vector", "sc", S.cnt("sc"))
        S.drain("vector")
        S.op("vector", lambda e: e.reciprocal(out=r1c[:, 0:NT], in_=r1c[:, 0:NT]))
        S.drain("vector")
        S.op("vector", lambda e: e.tensor_scalar(
            out=r2c[:, 0:NT], in0=T256b[:, 0:NT], scalar1=1e-5, scalar2=None,
            op0=ALU.add))
        S.drain("vector")
        S.op("vector", lambda e: e.reciprocal(out=r2c[:, 0:NT], in_=r2c[:, 0:NT]))
        S.drain("vector")
        S.op("vector", lambda e: e.tensor_tensor(
            out=r2c[:, 0:NT], in0=r2c[:, 0:NT], in1=T256b[:, 0:NT], op=ALU.mult))
        S.drain("vector")
        S.op("vector", lambda e: e.tensor_scalar(
            out=r2c[:, 0:NT], in0=r2c[:, 0:NT], scalar1=1e-5, scalar2=None,
            op0=ALU.add), "dv")
        S.wait("scalar", "dv", S.cnt("dv"))
        S.op("scalar", lambda e: e.activation(out=r2c[:, 0:NT], in_=r2c[:, 0:NT],
                                              func=AF.Sqrt), "sc")
        S.wait("vector", "sc", S.cnt("sc"))
        S.drain("vector")
        S.op("vector", lambda e: e.reciprocal(out=r2c[:, 0:NT], in_=r2c[:, 0:NT]))
        S.drain("vector")
        S.op("vector", lambda e: e.tensor_tensor(
            out=r12c[:, 0:NT], in0=r1c[:, 0:NT], in1=r2c[:, 0:NT], op=ALU.mult),
            "dv")
        dv_stats = S.cnt("dv")
        S.drain("vector")
        S.op("vector", lambda e: e.tensor_scalar(
            out=T256b[:, 0:NW], in0=T256b[:, 0:NW], scalar1=1e-5, scalar2=None,
            op0=ALU.add))
        S.drain("vector")
        S.op("vector", lambda e: e.reciprocal(out=T256b[:, 0:NW], in_=T256b[:, 0:NW]))
        S.drain("vector")
        S.op("vector", lambda e: e.tensor_tensor(
            out=st_mx[:, 0:NW], in0=st_mx[:, 0:NW], in1=T256b[:, 0:NW], op=ALU.mult))
        S.drain("vector")
        S.op("vector", lambda e: e.reduce_max(out=tcol[:, 2:3], in_=st_mx[:, 0:NW],
                                              axis=AX.X), "dv")
        S.wait("tensor", "dv", S.cnt("dv"))
        S.op("tensor", lambda e: e.transpose(p6[0:1, 0:128], tcol[:, 2:3], id32[:]),
             "pe")
        S.wait("vector", "pe", S.cnt("pe"))
        S.drain("vector")
        S.op("vector", lambda e: e.tensor_copy(out=trow[:], in_=p6[0:1, 0:128]))
        S.drain("vector")
        S.op("vector", lambda e: e.reduce_max(out=sca[0:1, 4:5], in_=trow[:],
                                              axis=AX.X), "dv")
        S.wait("scalar", "dv", S.cnt("dv"))
        S.op("scalar", lambda e: e.activation(out=sca[0:1, 4:5], in_=sca[0:1, 4:5],
                                              func=AF.Sqrt), "sc")
        S.wait("vector", "sc", S.cnt("sc"))
        S.drain("vector")
        S.op("vector", lambda e: e.tensor_scalar(
            out=sca[0:1, 4:5], in0=sca[0:1, 4:5], scalar1=1.0 / 1024.0,
            scalar2=None, op0=ALU.mult))
        S.drain("vector")
        S.op("vector", lambda e: e.tensor_scalar(
            out=sca[0:1, 5:6], in0=sca[0:1, 4:5], scalar1=1e-12, scalar2=None,
            op0=ALU.max))
        S.drain("vector")
        S.op("vector", lambda e: e.reciprocal(out=sca[0:1, 6:7], in_=sca[0:1, 5:6]))
        S.drain("vector")
        S.op("vector", lambda e: e.tensor_scalar(
            out=sca[0:1, 6:7], in0=sca[0:1, 6:7], scalar1=224.0, scalar2=None,
            op0=ALU.mult))
        S.drain("vector")
        S.op("vector", lambda e: e.tensor_tensor(
            out=sca[0:1, 7:8], in0=sca[0:1, 5:6], in1=sca[0:1, 2:3], op=ALU.mult))
        S.drain("vector")
        S.op("vector", lambda e: e.tensor_scalar(
            out=sca[0:1, 7:8], in0=sca[0:1, 7:8], scalar1=1.0 / (224.0 * 224.0),
            scalar2=None, op0=ALU.mult), "dv")
        S.wait("tensor", "dv", S.cnt("dv"))
        S.op("tensor", lambda e: e.matmul(p6[:, 0:1], onef[:], sca[0:1, 6:7],
                                          start=True, stop=True))
        S.op("tensor", lambda e: e.matmul(p6[:, 1:2], onef[:], sca[0:1, 7:8],
                                          start=True, stop=True), "pe")
        S.wait("vector", "pe", S.cnt("pe"))
        S.drain("vector")
        S.op("vector", lambda e: e.tensor_copy(out=sxcol[:], in_=p6[:, 0:1]))
        S.drain("vector")
        S.op("vector", lambda e: e.tensor_copy(out=ccol[:], in_=p6[:, 1:2]))
        S.drain("vector")
        S.op("vector", lambda e: e.tensor_scalar(
            out=D1[:], in0=id32[:], scalar1=sxcol[:], scalar2=None,
            op0=ALU.mult))
        S.drain("vector")
        S.op("vector", lambda e: e.reciprocal(out=sca[0:1, 0:1], in_=sca[0:1, 7:8]))
        S.wait("vector", "gl", gl_bf)
        S.drain("vector")
        S.op("vector", lambda e: e.tensor_scalar(
            out=bfrb[:], in0=bfrb[:], scalar1=sca[0:1, 0:1], scalar2=None,
            op0=ALU.mult), "dv")
        dv_scales = S.cnt("dv")

        if debug_stop == "passA":
            S.wait("sync", "dv", S.cnt("dv"))
            S.wait("sync", "sc", S.cnt("sc"))
            S.wait("sync", "pe", S.cnt("pe"))
            for i in range(T // 128):
                S.wait("sync", "sta", 16 * i)
                S.op("sync", lambda e, i=i: e.dma_start(
                    out=y_out[i * 128:(i + 1) * 128, :], in_=XN[:, 0:1024]),
                    "sta", 16)
            S.wait("sync", "sta", S.cnt("sta"))
            _emit_all()
            return nc

        S.wait("sync", "dv", dv_scales)
        S.wait("vector", "sc", S.cnt("sc"))
        LX = ["lxa", "lxb"]
        for sub in range(2):
            S.op("sync", lambda e, sub=sub: e.dma_start(
                out=XIN[0][:, sub * 1024:(sub + 1) * 1024],
                in_=x_own[sub * 128:(sub + 1) * 128, :]), "lxa", 16)

        bdv = {}
        bsc = {}
        hq_ready = [0] * 8
        def bank_wait(*banks):
            for b in banks:
                v = bdv.get(id(b), 0)
                if v:
                    S.wait("tensor", "dv", v)
                v = bsc.get(id(b), 0)
                if v:
                    S.wait("tensor", "sc", v)
        st_cnt = 0
        for m in range(NMAC):
            xin = XIN[m % 2]
            if m + 1 < NMAC:
                if m == 0:
                    S.wait("sync", "dv", dv_wfq)
                S.wait("sync", "gp", S.cnt("gp"))
                for sub in range(2):
                    S.op("sync", lambda e, m=m, sub=sub: e.dma_start(
                        out=XIN[(m + 1) % 2][:, sub * 1024:(sub + 1) * 1024],
                        in_=x_own[(m + 1) * 256 + sub * 128:
                                  (m + 1) * 256 + (sub + 1) * 128, :]),
                        LX[(m + 1) % 2], 16)

            if m == 0:
                xn_done = [0, 0]
                S.wait("vector", LX[0], 32)
                for sub in range(2):
                    S.op("vector", lambda e, sub=sub: e.tensor_scalar(
                        out=XN[:, sub * 1024:(sub + 1) * 1024],
                        in0=XIN[0][:, sub * 1024:(sub + 1) * 1024],
                        scalar1=mu1c[:, sub:sub + 1], scalar2=r1c[:, sub:sub + 1],
                        op0=ALU.subtract, op1=ALU.mult))
                    S.op("vector", lambda e, sub=sub: e.tensor_scalar(
                        out=D2T[:, sub * 128:(sub + 1) * 128], in0=id32[:],
                        scalar1=r2c[:, sub:sub + 1], scalar2=None,
                        op0=ALU.mult), "dv")
                    xn_done[sub] = S.cnt("dv")

            tb_rot = (p0, p1, p6, p7)
            tbi = 0
            sc_zt = 0
            for sub in range(2):
                S.wait("tensor", "dv", xn_done[sub])
                for part in ("z", "x"):
                    dst = ZT if part == "z" else XQ
                    rhs = (D2T[:, sub * 128:(sub + 1) * 128] if part == "z"
                           else D1[:])
                    for half in range(2):
                        bank = tb_rot[tbi % 4]
                        tbi += 1
                        bank_wait(bank)
                        pe_t = None
                        for v in range(4):
                            kk = half * 4 + v
                            pe_t = S.op(
                                "tensor", lambda e, bank=bank, v=v, sub=sub,
                                kk=kk, rhs=rhs:
                                e.matmul(bank[:, v * 128:(v + 1) * 128],
                                         XN[:, sub * 1024 + kk * 128:
                                            sub * 1024 + (kk + 1) * 128],
                                         rhs, start=True, stop=True),
                                "pe" if v == 3 else None)
                        S.wait("scalar", "pe", pe_t)
                        for v in range(4):
                            kk = half * 4 + v
                            S.op("scalar", lambda e, bank=bank, v=v, sub=sub,
                                 kk=kk, dst=dst:
                                 e.copy(out=dst[:, kk * 256 + sub * 128:
                                                kk * 256 + (sub + 1) * 128],
                                        in_=bank[:, v * 128:(v + 1) * 128]),
                                 "sc")
                        bsc[id(bank)] = S.cnt("sc")
                    if part == "z" and sub == 1:
                        sc_zt = S.cnt("sc")

            if m == 0 and debug_stop == "ph2":
                S.wait("sync", "dv", S.cnt("dv"))
                S.wait("sync", "sc", S.cnt("sc"))
                S.wait("sync", "pe", S.cnt("pe"))
                S.wait("sync", "gp", S.cnt("gp"))
                for i in range(T // 128):
                    S.wait("sync", "stb", 16 * i)
                    S.op("sync", lambda e, i=i: e.dma_start(
                        out=y_out[i * 128:(i + 1) * 128, :], in_=XN[:, 0:1024]),
                        "stb", 16)
                S.wait("sync", "stb", S.cnt("stb"))
                _emit_all()
                return nc

            S.wait("tensor", "sc", sc_zt)
            if m == 0:
                S.wait("tensor", "dv", dv_passA)
            for proj, Wn, brow_bf, Mden in (("q", "q", bqr, Mq2),
                                            ("k", "k", bkr, Mk2),
                                            ("g", "g", bgr, None)):
                for g in range(8):
                    bank = (p0, p1, p6, p7)[g % 4]
                    view = bank[:, (g % 8 // 4) * 256:(g % 8 // 4) * 256 + 256]
                    bank_wait(bank)
                    for kk in range(8):
                        S.op("tensor", lambda e, Wn=Wn, g=g, kk=kk, view=view:
                             e.matmul(view,
                                      Wb[Wn][:, kk * 1024 + g * 128:
                                             kk * 1024 + (g + 1) * 128],
                                      ZT[:, kk * 256:(kk + 1) * 256],
                                      start=(kk == 0), stop=False))
                    pe_g = S.op("tensor", lambda e, g=g, view=view, brow_bf=brow_bf:
                                e.matmul(view, brow_bf[0:1, g * 128:(g + 1) * 128],
                                         onebf[0:1, 0:256],
                                         start=False, stop=True), "pe")
                    if proj == "g":
                        S.wait("scalar", "pe", pe_g)
                        S.op("scalar", lambda e, g=g, view=view:
                             e.activation(out=GT[:, g * 256:(g + 1) * 256],
                                          in_=view, func=AF.Silu), "sc")
                        bsc[id(bank)] = S.cnt("sc")
                    elif proj == "q":
                        S.wait("vector", "pe", pe_g)
                        for c in range(2):
                            S.op("vector", lambda e, g=g, c=c, view=view, Mden=Mden:
                                 e.tensor_tensor(
                                     out=Q2[:, g * 256 + c * 128:
                                            g * 256 + (c + 1) * 128],
                                     in0=view[:, c * 128:(c + 1) * 128],
                                     in1=Mden[:, g * 128:(g + 1) * 128],
                                     op=ALU.mult), "dv")
                        bdv[id(bank)] = S.cnt("dv")
                    else:
                        S.wait("vector", "pe", pe_g)
                        for c in range(2):
                            R = (c * 8 + g) * 256
                            for ha in range(2):
                                ho = ha * 64
                                S.op("vector", lambda e, g=g, c=c, ha=ha, ho=ho,
                                     R=R, view=view, Mden=Mden:
                                     e.tensor_tensor(
                                         out=K2BD[ho:ho + 64,
                                                  R + ha * 128:R + (ha + 1) * 128],
                                         in0=view[ho:ho + 64,
                                                  c * 128:(c + 1) * 128],
                                         in1=Mden[ho:ho + 64,
                                                  g * 128:(g + 1) * 128],
                                         op=ALU.mult))
                            S.op("vector", lambda e, g=g, c=c, view=view:
                                 e.tensor_tensor(
                                     out=K3F[:, g * 256 + c * 128:
                                             g * 256 + (c + 1) * 128],
                                     in0=view[:, c * 128:(c + 1) * 128],
                                     in1=Mk3[:, g * 128:(g + 1) * 128],
                                     op=ALU.mult), "dv")
                        bdv[id(bank)] = S.cnt("dv")
            dv_qkg = S.cnt("dv")
            sc_g = S.cnt("sc")

            S.wait("tensor", "dv", dv_qkg)
            for batch in range(2):
                bank = (pta, ptb)[batch]
                bank_wait(bank)
                pe_t = None
                for v in range(8):
                    idx = batch * 8 + v
                    g, c = idx // 2, idx % 2
                    pe_t = S.op("tensor", lambda e, bank=bank, v=v, g=g, c=c:
                         e.transpose(bank[:, v * 128:(v + 1) * 128],
                                     K3F[:, g * 256 + c * 128:g * 256 + (c + 1) * 128],
                                     idbf[:]),
                         "pe" if v == 7 else None)
                S.wait("scalar", "pe", pe_t)
                for v in range(8):
                    idx = batch * 8 + v
                    g, c = idx // 2, idx % 2
                    S.op("scalar", lambda e, bank=bank, v=v, g=g, c=c:
                         e.copy(
                             out=K3T[:, (c * 8 + g) * 128:(c * 8 + g + 1) * 128],
                             in_=bank[:, v * 128:(v + 1) * 128]), "sc")
                bsc[id(bank)] = S.cnt("sc")

            for sub in range(2):
                bank_wait(p4, p5)
                for half, bank in ((0, p4), (1, p5)):
                    for kk in range(8):
                        S.op("tensor", lambda e, sub=sub, kk=kk, half=half, bank=bank:
                             e.matmul(bank[:, 0:512],
                                      ZT[:, kk * 256 + sub * 128:
                                         kk * 256 + (sub + 1) * 128],
                                      Wb["v"][:, kk * 1024 + half * 512:
                                              kk * 1024 + (half + 1) * 512],
                                      start=(kk == 0), stop=False))
                    pe_v = S.op("tensor", lambda e, half=half, bank=bank: e.matmul(
                        bank[:, 0:512], onebf[0:1, 0:128],
                        bvb[0:1, half * 512:(half + 1) * 512],
                        start=False, stop=True),
                        "pe" if half == 1 else None)
                S.wait("scalar", "pe", pe_v)
                for half, bank in ((0, p4), (1, p5)):
                    S.op("scalar", lambda e, sub=sub, half=half, bank=bank:
                         e.copy(
                             out=VT[:, sub * 1024 + half * 512:
                                    sub * 1024 + (half + 1) * 512],
                             in_=bank[:, 0:512]), "sc")
                bsc[id(p4)] = bsc[id(p5)] = S.cnt("sc")
            dv_v = S.cnt("dv")
            sc_v = S.cnt("sc")

            if m == 0 and debug_stop == "ph3":
                S.wait("sync", "dv", S.cnt("dv"))
                S.wait("sync", "sc", S.cnt("sc"))
                S.wait("sync", "pe", S.cnt("pe"))
                S.wait("sync", "gp", S.cnt("gp"))
                for i in range(T // 128):
                    S.wait("sync", "stb", 16 * i)
                    S.op("sync", lambda e, i=i: e.dma_start(
                        out=y_out[i * 128:(i + 1) * 128, :], in_=XN[:, 0:1024]),
                        "stb", 16)
                S.wait("sync", "stb", S.cnt("stb"))
                _emit_all()
                return nc

            S.wait("tensor", "dv", dv_v)
            S.wait("tensor", "sc", sc_v)
            if m == 0:
                S.wait("tensor", "gp", gp_init)
            units = [(c, g) for c in range(2) for g in range(8)]
            dv_am_u = [0] * 16

            def emit_A(idx):
                c, g = units[idx]
                pa_bank = (p6, p7)[idx % 2]
                amsl = (idx % 2) * 256
                qcol = g * 256 + c * 128
                bank_wait(pa_bank)
                pe_a = None
                R = (c * 8 + g) * 256
                for ha in range(2):
                    ac = ha * 128
                    S.op("tensor", lambda e, R=R, ha=ha, ac=ac,
                         pa_bank=pa_bank, qcol=qcol: e.matmul(
                             pa_bank[0:64, ac:ac + 128],
                             K2BD[:, R + ha * 128:R + ha * 128 + 64],
                             Q2[:, qcol:qcol + 128],
                             start=True, stop=True))
                    pe_a = S.op("tensor", lambda e, R=R, ha=ha, ac=ac,
                         pa_bank=pa_bank, qcol=qcol: e.matmul(
                             pa_bank[64:128, ac + 64:ac + 128],
                             K2BD[:, R + ha * 128 + 64:R + ha * 128 + 128],
                             Q2[:, qcol + 64:qcol + 128],
                             start=True, stop=True),
                         "pe" if ha == 1 else None)
                S.wait("vector", "pe", pe_a)
                for ha in range(2):
                    ac = ha * 128
                    S.op("vector", lambda e, pa_bank=pa_bank, ac=ac, amsl=amsl:
                         e.tensor_tensor(
                             out=AM[0:64, amsl + ac:amsl + ac + 128],
                             in0=pa_bank[0:64, ac:ac + 128],
                             in1=mUT[0:64, 0:128], op=ALU.mult))
                    S.op("vector", lambda e, pa_bank=pa_bank, ac=ac, amsl=amsl:
                         e.tensor_tensor(
                             out=AM[64:128, amsl + ac + 64:amsl + ac + 128],
                             in0=pa_bank[64:128, ac + 64:ac + 128],
                             in1=mUT[64:128, 64:128], op=ALU.mult), "dv")
                dv_am_u[idx] = S.cnt("dv")
                bdv[id(pa_bank)] = dv_am_u[idx]

            def emit_oU(idx):
                c, g = units[idx]
                po_bank = (p0, p1)[idx % 2]
                pu_bank = (p4, p5)[idx % 2]
                amsl = (idx % 2) * 256
                qcol = g * 256 + c * 128
                bank_wait(po_bank, pu_bank)
                if hq_ready[g]:
                    S.wait("tensor", "dv", hq_ready[g])
                S.wait("tensor", "dv", dv_am_u[idx])
                for ha in range(2):
                    h = 2 * g + ha
                    S.op("tensor", lambda e, g=g, ha=ha, qcol=qcol,
                         po_bank=po_bank: e.matmul(
                             po_bank[ha * 64:ha * 64 + 64, 0:128],
                             HQB[:, g * 128 + ha * 64:g * 128 + (ha + 1) * 64],
                             Q2[:, qcol:qcol + 128], start=True, stop=False))
                    S.op("tensor", lambda e, c=c, h=h, ha=ha, amsl=amsl,
                         po_bank=po_bank: e.matmul(
                             po_bank[ha * 64:ha * 64 + 64, 0:128],
                             VT[:, c * 1024 + h * 64:c * 1024 + (h + 1) * 64],
                             AM[:, amsl + ha * 128:amsl + (ha + 1) * 128],
                             start=False, stop=True))
                pe_ou = S.op("tensor", lambda e, g=g, c=c, pu_bank=pu_bank:
                             e.matmul(
                                 pu_bank[:, 0:128],
                                 K3T[:, (c * 8 + g) * 128:(c * 8 + g + 1) * 128],
                                 VT[:, c * 1024 + g * 128:c * 1024 + (g + 1) * 128],
                                 start=True, stop=True), "pe")
                S.wait("vector", "pe", pe_ou)
                S.op("vector", lambda e, po_bank=po_bank, qcol=qcol:
                     e.tensor_tensor(
                         out=OG[:, qcol:qcol + 128], in0=po_bank[:, 0:128],
                         in1=GT[:, qcol:qcol + 128], op=ALU.mult), "dv")
                bdv[id(po_bank)] = S.cnt("dv")
                for ha in range(2):
                    ho = ha * 64
                    S.op("vector", lambda e, g=g, ho=ho,
                         pu_bank=pu_bank:
                         e.scalar_tensor_tensor(
                             out=HQB[ho:ho + 64, g * 128 + ho:g * 128 + ho + 64],
                             in0=HQB[ho:ho + 64, g * 128 + ho:g * 128 + ho + 64],
                             scalar=a128c[ho:ho + 64, g:g + 1],
                             in1=pu_bank[ho:ho + 64, ho:ho + 64],
                             op0=ALU.mult, op1=ALU.add),
                         "dv" if ha == 1 else None)
                bdv[id(pu_bank)] = S.cnt("dv")
                hq_ready[g] = S.cnt("dv")

            for idx in range(16):
                emit_A(idx)
                if idx >= 1:
                    emit_oU(idx - 1)
            emit_oU(15)
            dv_rec = S.cnt("dv")

            if m == 0 and debug_stop == "ph4":
                S.wait("sync", "dv", S.cnt("dv"))
                S.wait("sync", "sc", S.cnt("sc"))
                S.wait("sync", "pe", S.cnt("pe"))
                S.wait("sync", "gp", S.cnt("gp"))
                for i in range(T // 128):
                    S.wait("sync", "stb", 16 * i)
                    S.op("sync", lambda e, i=i: e.dma_start(
                        out=y_out[i * 128:(i + 1) * 128, :], in_=XN[:, 0:1024]),
                        "stb", 16)
                S.wait("sync", "stb", S.cnt("stb"))
                _emit_all()
                return nc

            pe_y = [0, 0]
            pe_f = [0, 0]
            sc_si = [0, 0]
            nxt = m + 1 < NMAC

            def ymms(sub):
                for half, bank in ((0, p4), (1, p5)):
                    for kk in range(8):
                        S.op("tensor", lambda e, sub=sub, kk=kk, half=half,
                             bank=bank:
                             e.matmul(bank[:, 0:512],
                                      OG[:, kk * 256 + sub * 128:
                                         kk * 256 + (sub + 1) * 128],
                                      Wb["o"][:, kk * 1024 + half * 512:
                                              kk * 1024 + (half + 1) * 512],
                                      start=(kk == 0), stop=False))
                    pe_y[sub] = S.op(
                        "tensor", lambda e, half=half, bank=bank:
                        e.matmul(bank[:, 0:512], onebf[0:1, 0:128],
                                 bob[0:1, half * 512:(half + 1) * 512],
                                 start=False, stop=True),
                        "pe" if half == 1 else None)

            def fmms(sub, banks):
                for half, bank in ((0, banks[0]), (1, banks[1])):
                    for kk in range(8):
                        S.op("tensor", lambda e, sub=sub, kk=kk, half=half,
                             bank=bank:
                             e.matmul(bank[:, 0:512],
                                      XQ[:, kk * 256 + sub * 128:
                                         kk * 256 + (sub + 1) * 128],
                                      WFc[:, kk * 1024 + half * 512:
                                          kk * 1024 + (half + 1) * 512],
                                      start=(kk == 0), stop=False))
                    pe_f[sub] = S.op(
                        "tensor", lambda e, half=half, bank=bank:
                        e.matmul(bank[:, 0:512], onebf[0:1, 0:128],
                                 bfrb[0:1, half * 512:(half + 1) * 512],
                                 start=False, stop=True),
                        "pe" if half == 1 else None)

            def outadd(sub):
                S.wait("vector", "pe", pe_y[sub])
                S.wait("vector", ("sta", "stb")[sub], 16 * m)
                for half, bank in ((0, p4), (1, p5)):
                    S.op("vector", lambda e, sub=sub, half=half, bank=bank:
                         e.tensor_tensor(
                             out=OUT[:, sub * 1024 + half * 512:
                                    sub * 1024 + (half + 1) * 512],
                             in0=bank[:, 0:512],
                             in1=XN[:, sub * 1024 + half * 512:
                                    sub * 1024 + (half + 1) * 512],
                             op=ALU.add), "dv")
                bdv[id(p4)] = bdv[id(p5)] = S.cnt("dv")
                return S.cnt("dv")

            def fevac(sub, banks):
                S.wait("vector", "pe", pe_f[sub])
                for half, bank in ((0, banks[0]), (1, banks[1])):
                    S.op("vector", lambda e, sub=sub, half=half, bank=bank:
                         e.tensor_scalar(
                             out=FB[:, sub * 1024 + half * 512:
                                    sub * 1024 + (half + 1) * 512],
                             in0=bank[:, 0:512], scalar1=ccol[:],
                             scalar2=None, op0=ALU.mult), "dv")
                bdv[id(banks[0])] = bdv[id(banks[1])] = S.cnt("dv")
                S.wait("scalar", "dv", S.cnt("dv"))
                sc_si[sub] = S.op("scalar", lambda e, sub=sub: e.activation(
                    out=FB[:, sub * 1024:(sub + 1) * 1024],
                    in_=FB[:, sub * 1024:(sub + 1) * 1024], func=AF.Silu), "sc")

            S.wait("tensor", "dv", dv_rec)
            bank_wait(p4, p5)
            ymms(0)
            bank_wait(p6, p7)
            fmms(0, (p6, p7))
            dv_oa = [0, 0]
            dv_oa[0] = outadd(0)
            if nxt:
                xn_done = [0, 0]
                S.wait("vector", LX[(m + 1) % 2], 32 * ((m + 1) // 2 + 1))
                S.drain("vector")
                ti = (m + 1) * 2
                S.op("vector", lambda e, ti=ti, xn_in=XIN[(m + 1) % 2]:
                     e.tensor_scalar(
                         out=XN[:, 0:1024], in0=xn_in[:, 0:1024],
                         scalar1=mu1c[:, ti:ti + 1], scalar2=r1c[:, ti:ti + 1],
                         op0=ALU.subtract, op1=ALU.mult))
                S.op("vector", lambda e, ti=ti: e.tensor_scalar(
                    out=D2T[:, 0:128], in0=id32[:],
                    scalar1=r2c[:, ti:ti + 1], scalar2=None, op0=ALU.mult),
                    "dv")
                xn_done[0] = S.cnt("dv")
            fevac(0, (p6, p7))
            bank_wait(p4, p5)
            ymms(1)
            dv_oa[1] = outadd(1)
            if nxt:
                ti = (m + 1) * 2 + 1
                S.drain("vector")
                S.op("vector", lambda e, ti=ti, xn_in=XIN[(m + 1) % 2]:
                     e.tensor_scalar(
                         out=XN[:, 1024:2048], in0=xn_in[:, 1024:2048],
                         scalar1=mu1c[:, ti:ti + 1], scalar2=r1c[:, ti:ti + 1],
                         op0=ALU.subtract, op1=ALU.mult))
                S.op("vector", lambda e, ti=ti: e.tensor_scalar(
                    out=D2T[:, 128:256], in0=id32[:],
                    scalar1=r2c[:, ti:ti + 1], scalar2=None, op0=ALU.mult),
                    "dv")
                xn_done[1] = S.cnt("dv")
            bank_wait(p4, p5)
            fmms(1, (p4, p5))
            fevac(1, (p4, p5))
            S.drain("vector")
            for sub in range(2):
                S.wait("vector", "sc", sc_si[sub])
                dv_sg = S.op("vector", lambda e, sub=sub: e.tensor_tensor(
                    out=OUT[:, sub * 1024:(sub + 1) * 1024],
                    in0=OUT[:, sub * 1024:(sub + 1) * 1024],
                    in1=FB[:, sub * 1024:(sub + 1) * 1024], op=ALU.mult), "dv")
                S.wait("gpsimd", "dv", dv_sg)
                gp_o = S.op("gpsimd", lambda e, sub=sub, xin=xin: e.tensor_tensor(
                    out=OUT[:, sub * 1024:(sub + 1) * 1024],
                    in0=OUT[:, sub * 1024:(sub + 1) * 1024],
                    in1=xin[:, sub * 1024:(sub + 1) * 1024], op=ALU.add), "gp")
                S.wait("sync", "gp", gp_o)
                S.op("sync", lambda e, m=m, sub=sub: e.dma_start(
                    out=y_out[m * 256 + sub * 128:m * 256 + (sub + 1) * 128, :],
                    in_=OUT[:, sub * 1024:(sub + 1) * 1024]),
                    ("sta", "stb")[sub], 16)
                st_cnt += 1

        S.wait("sync", "sta", S.cnt("sta"))
        S.wait("sync", "stb", S.cnt("stb"))

        _emit_all()

    return nc


def _host_inputs(inputs, T=4096):
    x = np.asarray(inputs["L0_attn_x"], dtype=np.float32)
    B = x.shape[0]
    tp = lambda a: np.ascontiguousarray(np.asarray(a, np.float32).T)
    col8 = lambda v: np.ascontiguousarray(
        np.asarray(v, np.float32).reshape(8, 128).T)
    wts = {"q": tp(inputs["L0_attn_WQ"]), "k": tp(inputs["L0_attn_WK"]),
           "v": tp(inputs["L0_attn_WV"]), "g": tp(inputs["L0_attn_Wg"]),
           "o": tp(inputs["L0_attn_WO"]), "f": tp(inputs["L0_ffn_weight"])}
    tt = np.arange(128, dtype=np.float32)
    e_q = np.tile(tt - 63.0, (128, 1)).astype(np.float32)
    e_k = np.tile(63.0 - tt, (128, 1)).astype(np.float32)
    e_k3 = np.tile(191.0 - tt, (128, 1)).astype(np.float32)
    maskut = (np.arange(128)[:, None] <= np.arange(128)[None, :]).astype(np.float32)
    common = {
        "wt_q": wts["q"], "wt_k": wts["k"], "wt_v": wts["v"], "wt_g": wts["g"],
        "wt_o": wts["o"], "wt_f": wts["f"],
        "b_q_c": col8(inputs["L0_attn_bQ"]),
        "b_k_c": col8(inputs["L0_attn_bK"]),
        "b_g_c": col8(inputs["L0_attn_bg"]),
        "b_v_r": np.asarray(inputs["L0_attn_bV"], np.float32).reshape(1, D),
        "b_o_r": np.asarray(inputs["L0_attn_bO"], np.float32).reshape(1, D),
        "b_f_r": np.asarray(inputs["L0_ffn_bias"], np.float32).reshape(1, D),
        "gamma_c": col8(inputs["L0_attn_ln_gamma"]),
        "beta_c": col8(inputs["L0_attn_ln_beta"]),
        "alpha2h": np.ascontiguousarray(
            np.asarray(inputs["L0_attn_alpha_logit"], np.float32).reshape(8, 128).T),
        "ident32": np.eye(128, dtype=np.float32),
        "identbf": np.eye(128).astype(ml_dtypes.bfloat16),
        "e_q": e_q, "e_k": e_k, "e_k3": e_k3, "maskut": maskut,
        "ones_bf": np.ones((1, 256), dtype=ml_dtypes.bfloat16),
        "ones_f32": np.ones((1, 128), dtype=np.float32),
    }
    in_maps = []
    for c in range(B):
        rest = np.ascontiguousarray(
            np.concatenate([x[b] for b in range(B) if b != c], axis=0))
        m = dict(common)
        m["x_own"] = np.ascontiguousarray(x[c])
        m["x_rest"] = rest
        in_maps.append(m)
    return in_maps


def kernel(**inputs):
    T = 4096
    in_maps = _host_inputs(inputs, T)
    nc = build_core(T=T, NB_REST=7)
    res = run_bass_kernel_spmd(nc, in_maps, core_ids=list(range(8)))
    out = np.stack([r["y_out"] for r in res.results], axis=0)
    return out.astype(np.float32)
